# version 03
# baseline (speedup 1.0000x reference)
import math
import numpy as np
from contextlib import ExitStack

import concourse.bass as bass
import concourse.mybir as mybir
from concourse.bass_utils import run_bass_kernel_spmd

F32 = mybir.dt.float32
BF16 = mybir.dt.bfloat16
I16 = mybir.dt.int16
I32 = mybir.dt.int32
AF = mybir.ActivationFunctionType
ALU = mybir.AluOpType
AX = mybir.AxisListType

D = 2048
T = 2048
NH = 16
DK = 128
HF = 2816
NE = 8
CAP = 2048
NT = D // 128
SM_SCALE = 1.0 / math.sqrt(DK)
EXP_BIAS = -12.0

ENGINES = ("t", "v", "s", "g", "y")


class Prog:

    def __init__(self, nc):
        self.nc = nc
        self.ops = {e: [] for e in ENGINES}
        self.last_writer = {}
        self.readers = {}
        self.cidx = {e: 0 for e in ENGINES}
        self.didx = {e: 0 for e in ENGINES}
        self.referenced = {e: set() for e in ENGINES}
        self.vclock = {e: {} for e in ENGINES}
        self.last_tok = {e: None for e in ENGINES}
        self.pending_barrier = []

    def _dep(self, waits, ceng, tok):
        peng, kind, idx = tok
        if peng == ceng and kind == "c" and ceng == "t":
            return
        vc = self.vclock[ceng]
        key = (peng, kind)
        if vc.get(key, -1) >= idx:
            return
        vc[key] = idx
        waits.append(tok)
        if kind == "c":
            self.referenced[peng].add(idx)

    def _record(self, eng, kind, fn, reads, writes):
        waits = []
        for t in self.pending_barrier:
            if t is not None:
                self._dep(waits, eng, t)
        for k in reads:
            t = self.last_writer.get(k)
            if t is not None:
                self._dep(waits, eng, t)
        for k in writes:
            t = self.last_writer.get(k)
            if t is not None:
                self._dep(waits, eng, t)
            for t2 in self.readers.get(k, ()):
                self._dep(waits, eng, t2)
        idx = self.cidx[eng] if kind == "c" else self.didx[eng]
        tok = (eng, kind, idx)
        for k in reads:
            self.readers.setdefault(k, []).append(tok)
        for k in writes:
            self.last_writer[k] = tok
            self.readers[k] = []
        self.ops[eng].append((fn, kind, waits))
        self.last_tok[eng] = tok
        if kind == "c":
            self.cidx[eng] += 1
        else:
            self.didx[eng] += 1
        return tok

    def op(self, eng, fn, reads=(), writes=()):
        return self._record(eng, "c", fn, list(reads), list(writes))

    def dma(self, eng, fn, reads=(), writes=()):
        return self._record(eng, "d", fn, list(reads), list(writes))

    def barrier(self):
        self.pending_barrier = [self.last_tok[e] for e in ENGINES]

    NDL = 8

    def emit(self, stack):
        nc = self.nc
        csem = {e: stack.enter_context(nc.semaphore(f"c_{e}")) for e in ENGINES}
        dsem = {e: [stack.enter_context(nc.semaphore(f"d_{e}{i}"))
                    for i in range(self.NDL)]
                for e in ENGINES if self.didx[e] > 0}
        cval = {}
        for e in ENGINES:
            vals, v, ci = {}, 0, 0
            for (fn, kind, _) in self.ops[e]:
                if kind != "c":
                    continue
                if ci in self.referenced[e]:
                    v += 1
                vals[ci] = v
                ci += 1
            cval[e] = vals
        block = stack.enter_context(nc.Block())

        def make_section(e):
            def section(h):
                ci = 0
                di = 0
                for fn, kind, waits in self.ops[e]:
                    for (peng, pkind, pidx) in waits:
                        if pkind == "c":
                            h.wait_ge(csem[peng], cval[peng][pidx])
                        else:
                            h.wait_ge(dsem[peng][pidx % self.NDL],
                                      16 * (pidx // self.NDL + 1))
                    inst = fn(h)
                    if kind == "d":
                        inst.then_inc(dsem[e][di % self.NDL], 16)
                        di += 1
                    else:
                        if ci in self.referenced[e]:
                            inst.then_inc(csem[e], 1)
                        ci += 1
                for i in range(min(self.NDL, self.didx[e])):
                    cnt = (self.didx[e] - i + self.NDL - 1) // self.NDL
                    h.wait_ge(dsem[e][i], 16 * cnt)
            return section

        block.tensor(make_section("t"))
        block.vector(make_section("v"))
        block.scalar(make_section("s"))
        block.gpsimd(make_section("g"))
        block.sync(make_section("y"))


def rowkeys(base, n=NT):
    return [f"{base}:{i}" for i in range(n)]



def build_l1():
    nc = bass.Bass("TRN2", target_bir_lowering=False)
    X = nc.dram_tensor("x", [T, D], F32, kind="ExternalInput").ap()
    WT = {p: nc.dram_tensor(f"w{p}T", [D, D], F32, kind="ExternalInput").ap()
          for p in "qkvo"}
    G = {p: nc.dram_tensor(f"g{p}", [1, D], F32, kind="ExternalInput").ap()
         for p in "qkvo"}
    H1 = nc.dram_tensor("h1", [T, D], BF16, kind="ExternalOutput").ap()

    R = {p: nc.dram_tensor(f"r_{p}", [T, D], BF16, kind="Internal").ap()
         for p in "qkvo"}
    PD = {"q": nc.dram_tensor("q_tok", [T, D], BF16, kind="Internal").ap(),
          "k": nc.dram_tensor("k_tok", [T, D], BF16, kind="Internal").ap(),
          "v": nc.dram_tensor("v_tok", [T, D], BF16, kind="Internal").ap()}
    OTD = nc.dram_tensor("outT", [D, T], BF16, kind="Internal").ap()
    ZD = nc.dram_tensor("zd", [NH, T], F32, kind="Internal").ap()

    stack = ExitStack()
    with stack:
        SB = {}
        P = Prog(nc)

        uid = [0]

        def sb(st, name, shape, dt=F32):
            uid[0] += 1
            t = st.enter_context(nc.sbuf_tensor(f"{name}_{uid[0]}", shape, dt))
            SB[name] = t
            return t

        def ps(st, name, shape):
            uid[0] += 1
            return st.enter_context(
                nc.psum_tensor(f"{name}_{uid[0]}", shape, F32))

        ones_bf = sb(stack, "ones_bf", [128, 1], BF16)
        ones_fc = sb(stack, "ones_fc", [128, 1])
        ones_row = sb(stack, "ones_row", [1, 128])
        b1285 = sb(stack, "b1285", [128, 1])
        bneg = sb(stack, "bneg", [128, 1])
        for nm in ("c_eps6", "c_eps5", "c_invD", "c_neg05", "c_pos05",
                   "c_smscale"):
            sb(stack, nm, [128, 1])
        for nm, val in (("ones_bf", 1.0), ("ones_fc", 1.0), ("ones_row", 1.0),
                        ("b1285", 128.5), ("bneg", EXP_BIAS),
                        ("c_eps6", 1e-6), ("c_eps5", 1e-5),
                        ("c_invD", 1.0 / D), ("c_neg05", -0.5),
                        ("c_pos05", 0.5), ("c_smscale", SM_SCALE)):
            P.op("v", (lambda nm, val: lambda h: h.memset(SB[nm][:], val))(nm, val),
                 [], [nm])
        for p in "qkvo":
            sb(stack, f"gbc_{p}", [128, D], BF16)
            sb(stack, f"iw_{p}", [128, 1])
            sb(stack, f"thr_{p}", [128, 1])
            sb(stack, f"nthr_{p}", [128, 1])
            sb(stack, f"osc_{p}", [128, NT])
        xb = [sb(stack, f"xb{i}", [128, D]) for i in range(2)]
        iorow = sb(stack, "iorow", [128, 128])
        iocol = sb(stack, "iocol", [128, 128])
        idmat = sb(stack, "idmat", [128, 128])
        P.op("g", lambda h: h.iota(iorow[:], [[1, 128]], base=0,
                                   channel_multiplier=0,
                                   allow_small_or_imprecise_dtypes=True),
             [], ["iorow"])
        P.op("g", lambda h: h.iota(iocol[:], [[0, 128]], base=0,
                                   channel_multiplier=1,
                                   allow_small_or_imprecise_dtypes=True),
             [], ["iocol"])
        P.op("v", lambda h: h.tensor_tensor(idmat[:], iorow[:], iocol[:],
                                            ALU.is_equal),
             ["iorow", "iocol"], ["idmat"])
        wbuf = [sb(stack, f"wbuf{i}", [128, D]) for i in range(2)]
        scr = sb(stack, "scr", [128, D])
        for nm in ("ssq", "lrow", "irms", "rms5", "mx", "mc", "rc", "qs",
                   "acol", "smean", "ssq2", "msq", "mean", "mean2", "var",
                   "rstd"):
            sb(stack, nm, [128, 1])
        tot_sb = sb(stack, "tot_sb", [1, 4])

        st0 = ExitStack()
        pg = ps(st0, "pg", [128, 512])
        ptot = ps(st0, "ptot", [1, 4])
        pbc = ps(st0, "pbc", [128, 4])
        growb = sb(st0, "growb", [1, D])
        for p in "qkvo":
            P.dma("y", (lambda p: lambda h: h.dma_start(SB["growb"][:], G[p]))(p),
                  ["growb_r"], ["growb"])
            for j in range(4):
                P.op("t", (lambda p, j: lambda h: h.matmul(
                    pg[:], ones_row[:], SB["growb"][:, j * 512:(j + 1) * 512],
                    start=True, stop=True))(p, j),
                    ["ones_row", "growb"], ["pg"])
                P.op("v", (lambda p, j: lambda h: h.tensor_copy(
                    SB[f"gbc_{p}"][:, j * 512:(j + 1) * 512], pg[:]))(p, j),
                    ["pg"], [f"gbc_{p}", "growb_r"])
        for pi, p in enumerate("qkvo"):
            sb(st0, f"asum_{p}", [128, NT]) if f"asum_{p}" not in SB else None
            for cc in range(NT):
                b, bk = wbuf[cc % 2], f"wbuf{cc % 2}"
                P.dma("y", (lambda p, cc, b: lambda h: h.dma_start(
                    b[:], WT[p][cc * 128:(cc + 1) * 128, :]))(p, cc, b),
                    [], [bk])
                P.op("s", (lambda p, cc, b: lambda h: h.activation(
                    scr[:], b[:], AF.Abs,
                    accum_out=SB[f"asum_{p}"][:, cc:cc + 1]))(p, cc, b),
                    [bk], ["scr", f"asum_{p}"])
            P.op("v", (lambda p: lambda h: h.reduce_sum(
                SB["acol"][:], SB[f"asum_{p}"][:], AX.X))(p),
                [f"asum_{p}"], ["acol"])
            P.op("t", (lambda pi: lambda h: h.matmul(
                ptot[0:1, pi:pi + 1], SB["acol"][:], ones_fc[:],
                start=True, stop=True))(pi),
                ["acol", "ones_fc"], ["ptot"])
        P.op("v", lambda h: h.tensor_copy(tot_sb[:], ptot[:]),
             ["ptot"], ["tot_sb"])
        for pi in range(4):
            P.op("t", (lambda pi: lambda h: h.matmul(
                pbc[:, pi:pi + 1], ones_row[:], tot_sb[0:1, pi:pi + 1],
                start=True, stop=True))(pi),
                ["tot_sb", "ones_row"], ["pbc"])
        for pi, p in enumerate("qkvo"):
            P.op("v", (lambda p, pi: lambda h: h.tensor_scalar(
                SB[f"iw_{p}"][:], pbc[:, pi:pi + 1], 1.0 / (D * D), 1e-5,
                ALU.mult, ALU.max))(p, pi), ["pbc"], [f"iw_{p}"])
            P.op("v", (lambda p: lambda h: h.tensor_scalar_mul(
                SB[f"thr_{p}"][:], SB[f"iw_{p}"][:], 0.5))(p),
                [f"iw_{p}"], [f"thr_{p}"])
            P.op("v", (lambda p: lambda h: h.tensor_scalar_mul(
                SB[f"nthr_{p}"][:], SB[f"iw_{p}"][:], -0.5))(p),
                [f"iw_{p}"], [f"nthr_{p}"])

        def quant_block(stq, tag):
            for nm, shp, dt in ((f"t1{tag}", [128, D], F32),
                                (f"f{tag}", [128, D], F32),
                                (f"u16{tag}", [128, D], I16),
                                (f"corr{tag}", [128, D], F32),
                                (f"rbuf{tag}", [128, D], BF16),
                                (f"mx{tag}", [128, 1], F32),
                                (f"mc{tag}", [128, 1], F32),
                                (f"rc{tag}", [128, 1], F32),
                                (f"qs{tag}", [128, 1], F32)):
                sb(stq, nm, shp, dt)

        def quant_rows(src_key, src_ap, p, tag, tc, rdst_key):
            t1k = f"t1{tag}"
            mxk, mck, rck, qsk = (f"mx{tag}", f"mc{tag}", f"rc{tag}",
                                  f"qs{tag}")
            P.op("v", (lambda: lambda h: h.tensor_tensor(
                SB[t1k][:], src_ap[:], SB[f"gbc_{p}"][:], ALU.mult))(),
                [src_key, f"gbc_{p}"], [t1k])
            P.op("v", (lambda: lambda h: h.reduce_max(
                SB[mxk][:], SB[t1k][:], AX.X, apply_absolute_value=True))(),
                [t1k], [mxk])
            P.op("v", (lambda: lambda h: h.tensor_tensor(
                SB[mck][:], SB[mxk][:], SB["rms5"][:], ALU.max))(),
                [mxk, "rms5"], [mck])
            P.op("v", (lambda: lambda h: h.reciprocal(
                SB[rck][:], SB[mck][:]))(), [mck], [rck])
            P.op("v", (lambda: lambda h: h.tensor_scalar_mul(
                SB[qsk][:], SB[rck][:], 127.0))(), [rck], [qsk])
            P.op("v", (lambda p, tc: lambda h: h.tensor_tensor(
                SB[f"osc_{p}"][:, tc:tc + 1], SB[mck][:], SB["irms"][:],
                ALU.mult))(p, tc),
                [mck, "irms"], [f"osc_{p}"])
            P.op("s", (lambda: lambda h: h.activation(
                SB[f"f{tag}"][:], SB[t1k][:], AF.Identity,
                bias=b1285[:, 0:1], scale=SB[qsk][:, 0:1]))(),
                [t1k, qsk, "b1285"], [f"f{tag}"])
            P.op("s", (lambda: lambda h: h.activation(
                SB[f"u16{tag}"][:], SB[t1k][:], AF.Identity,
                bias=b1285[:, 0:1], scale=SB[qsk][:, 0:1]))(),
                [t1k, qsk, "b1285"], [f"u16{tag}"])
            P.op("v", (lambda: lambda h: h.tensor_tensor(
                SB[f"corr{tag}"][:], SB[f"u16{tag}"][:], SB[f"f{tag}"][:],
                ALU.is_gt))(),
                [f"u16{tag}", f"f{tag}"], [f"corr{tag}"])
            P.op("v", (lambda: lambda h: h.scalar_tensor_tensor(
                SB[f"rbuf{tag}"][:], SB[f"u16{tag}"][:], -128.0,
                SB[f"corr{tag}"][:], ALU.add, ALU.subtract))(),
                [f"u16{tag}", f"corr{tag}"], [f"rbuf{tag}"])
            P.dma("y", (lambda p, tc: lambda h: h.dma_start(
                R[p][tc * 128:(tc + 1) * 128, :], SB[f"rbuf{tag}"][:]))(p, tc),
                [f"rbuf{tag}"], [f"r_{p}:{tc}"])

        def rms_stats(src_key, src_ap):
            P.op("s", (lambda: lambda h: h.activation(
                scr[:], src_ap[:], AF.Square, accum_out=SB["ssq"][:]))(),
                [src_key], ["scr", "ssq"])
            P.op("s", lambda h: h.activation(SB["lrow"][:], SB["ssq"][:],
                                             AF.Ln, bias=SB["c_eps6"][:, 0:1],
                                             scale=SB["c_invD"][:, 0:1]),
                 ["ssq", "c_eps6", "c_invD"], ["lrow"])
            P.op("s", lambda h: h.activation(SB["irms"][:], SB["lrow"][:],
                                             AF.Exp,
                                             scale=SB["c_neg05"][:, 0:1]),
                 ["lrow", "c_neg05"], ["irms"])
            P.op("s", lambda h: h.activation(SB["rms5"][:], SB["lrow"][:],
                                             AF.Exp,
                                             scale=SB["c_pos05"][:, 0:1]),
                 ["lrow", "c_pos05"], ["rms5_raw"])
            P.op("v", lambda h: h.tensor_scalar_mul(SB["rms5"][:],
                                                    SB["rms5"][:], 1e-5),
                 ["rms5_raw"], ["rms5"])

        st1 = ExitStack()
        quant_block(st1, "A")
        quant_block(st1, "B")
        for tc in range(NT):
            b, bk = xb[tc % 2], f"xb{tc % 2}"
            P.dma("y", (lambda tc, b: lambda h: h.dma_start(
                b[:], X[tc * 128:(tc + 1) * 128, :]))(tc, b), [], [bk])
            rms_stats(bk, b)
            for pj, p in enumerate("qkv"):
                quant_rows(bk, b, p, "AB"[(tc * 3 + pj) % 2], tc, None)
        for p in "qkv":
            P.op("v", (lambda p: lambda h: h.tensor_scalar(
                SB[f"osc_{p}"][:], SB[f"osc_{p}"][:],
                SB[f"iw_{p}"][:, 0:1], 1.0 / 127.0, ALU.mult, ALU.mult))(p),
                [f"osc_{p}", f"iw_{p}"], [f"osc_{p}"])
        P.barrier()
        st1.close()
        st0.close()

        def projection(stp, p, dst, osc_key, epilogue=None):
            rT = [sb(stp, f"rT{i}", [128, D], BF16) for i in range(NT)]
            wt = [sb(stp, f"wt{i}", [128, D], BF16) for i in range(NT)]
            ltm = sb(stp, "ltm", [128, D], BF16)
            pmm = [ps(stp, f"pmm{i}", [128, 512]) for i in range(2)]
            evb = ([sb(stp, f"evb{i}", [128, 512], BF16) for i in range(2)]
                   if epilogue is None else None)
            for cc in range(NT):
                P.dma("y", (lambda p, cc: lambda h: h.dma_start(
                    rT[cc][:], R[p][:, cc * 128:(cc + 1) * 128],
                    transpose=True))(p, cc),
                    rowkeys(f"r_{p}"), [f"rT{cc}"])
            for cc in range(NT):
                b, bk = wbuf[cc % 2], f"wbuf{cc % 2}"
                P.dma("y", (lambda p, cc, b: lambda h: h.dma_start(
                    b[:], WT[p][cc * 128:(cc + 1) * 128, :]))(p, cc, b),
                    [], [bk])
                P.op("v", (lambda p, b: lambda h: h.tensor_single_scalar(
                    ltm[:], b[:], SB[f"nthr_{p}"][:, 0:1], ALU.is_lt))(p, b),
                    [bk, f"nthr_{p}"], ["ltm"])
                P.op("v", (lambda p, cc, b: lambda h: h.scalar_tensor_tensor(
                    wt[cc][:], b[:], SB[f"thr_{p}"][:, 0:1], ltm[:],
                    ALU.is_gt, ALU.subtract))(p, cc, b),
                    [bk, f"thr_{p}", "ltm"], [f"wt{cc}"])
            for tc in range(NT):
                for ob in range(4):
                    pi = (tc * 4 + ob) % 2
                    for cc in range(NT):
                        P.op("t", (lambda tc, ob, cc, pi: lambda h: h.matmul(
                            pmm[pi][:], rT[cc][:, tc * 128:(tc + 1) * 128],
                            wt[cc][:, ob * 512:(ob + 1) * 512],
                            start=(cc == 0), stop=(cc == NT - 1)))(tc, ob, cc, pi),
                            [f"rT{cc}", f"wt{cc}"], [f"pmm{pi}"])
                    if epilogue is None:
                        eb, ek = evb[pi], f"evb{pi}"
                        P.op("s", (lambda tc, pi, eb: lambda h: h.activation(
                            eb[:], pmm[pi][:], AF.Copy,
                            scale=SB[osc_key][:, tc:tc + 1]))(tc, pi, eb),
                            [f"pmm{pi}", osc_key], [ek])
                        P.dma("y", (lambda p, tc, ob, eb: lambda h: h.dma_start(
                            dst[tc * 128:(tc + 1) * 128,
                                ob * 512:(ob + 1) * 512], eb[:]))(p, tc, ob, eb),
                            [ek], [f"dst_{p}:{tc}"])
                    else:
                        epilogue(tc, ob, pi, pmm)

        for p in "kvq":
            stp = ExitStack()
            projection(stp, p, PD[p], f"osc_{p}")
            P.barrier()
            stp.close()

        st2 = ExitStack()
        vres = [sb(st2, f"vres{i}", [128, D], BF16) for i in range(NT)]
        qt = [sb(st2, f"qt{i}", [128, T], BF16) for i in range(2)]
        kt = [sb(st2, f"kt{i}", [128, T], BF16) for i in range(2)]
        expt = [sb(st2, f"expt{i}", [128, 1024], BF16) for i in range(NT)]
        oevb = [sb(st2, f"oevb{i}", [128, 1024], BF16) for i in range(2)]
        psc = [ps(st2, f"psc{i}", [128, 1024]) for i in range(2)]
        pot = ps(st2, "pot", [128, 1024])
        pzr = ps(st2, "pzr", [1, 1024])
        zsb = sb(st2, "zsb", [1, 1024])
        for cc in range(NT):
            P.dma("y", (lambda cc: lambda h: h.dma_start(
                vres[cc][:], PD["v"][cc * 128:(cc + 1) * 128, :]))(cc),
                [f"dst_v:{cc}"], [f"vres{cc}"])
        for hh in range(NH):
            hb = hh % 2
            P.dma("y", (lambda hh, hb: lambda h: h.dma_start(
                qt[hb][:], PD["q"][:, hh * 128:(hh + 1) * 128],
                transpose=True))(hh, hb), rowkeys("dst_q"), [f"qt{hb}"])
            P.dma("y", (lambda hh, hb: lambda h: h.dma_start(
                kt[hb][:], PD["k"][:, hh * 128:(hh + 1) * 128],
                transpose=True))(hh, hb), rowkeys("dst_k"), [f"kt{hb}"])
            for qh in range(2):
                for sc in range(NT):
                    sp = sc % 2
                    for j in range(2):
                        P.op("t", (lambda hb, qh, sc, sp, j: lambda h: h.matmul(
                            psc[sp][:, j * 512:(j + 1) * 512],
                            kt[hb][:, sc * 128:(sc + 1) * 128],
                            qt[hb][:, qh * 1024 + j * 512:
                                    qh * 1024 + (j + 1) * 512],
                            start=True, stop=True))(hb, qh, sc, sp, j),
                            [f"kt{hb}", f"qt{hb}"], [f"psc{sp}"])
                    P.op("s", (lambda sc, sp: lambda h: h.activation(
                        expt[sc][:], psc[sp][:], AF.Exp,
                        bias=bneg[:, 0:1],
                        scale=SB["c_smscale"][:, 0:1]))(sc, sp),
                        [f"psc{sp}", "bneg", "c_smscale"], [f"expt{sc}"])
                    for j in range(2):
                        P.op("t", (lambda hh, sc, j: lambda h: h.matmul(
                            pot[:, j * 512:(j + 1) * 512],
                            vres[sc][:, hh * 128:(hh + 1) * 128],
                            expt[sc][:, j * 512:(j + 1) * 512],
                            start=(sc == 0), stop=(sc == NT - 1)))(hh, sc, j),
                            [f"vres{sc}", f"expt{sc}"], ["pot"])
                for sc in range(NT):
                    for j in range(2):
                        P.op("t", (lambda sc, j: lambda h: h.matmul(
                            pzr[0:1, j * 512:(j + 1) * 512],
                            ones_bf[:],
                            expt[sc][:, j * 512:(j + 1) * 512],
                            start=(sc == 0), stop=(sc == NT - 1)))(sc, j),
                            [f"expt{sc}", "ones_bf"], ["pzr"])
                P.op("v", lambda h: h.tensor_copy(zsb[:], pzr[:]),
                     ["pzr"], ["zsb"])
                P.dma("y", (lambda hh, qh: lambda h: h.dma_start(
                    ZD[hh:hh + 1, qh * 1024:(qh + 1) * 1024], zsb[:]))(hh, qh),
                    ["zsb"], ["zd_d"])
                ob, obk = oevb[qh], f"oevb{qh}"
                P.op("s", (lambda ob: lambda h: h.activation(
                    ob[:], pot[:], AF.Copy))(ob), ["pot"], [obk])
                P.dma("y", (lambda hh, qh, ob: lambda h: h.dma_start(
                    OTD[hh * 128:(hh + 1) * 128, qh * 1024:(qh + 1) * 1024],
                    ob[:]))(hh, qh, ob), [obk], [f"outT_d:{hh}"])
        P.barrier()
        st2.close()

        st3 = ExitStack()
        quant_block(st3, "A")
        zr16 = sb(st3, "zr16", [NH, T])
        zbc = sb(st3, "zbc", [128, NH])
        zb = sb(st3, "zb", [128, D])
        outm = sb(st3, "outm", [128, D], BF16)
        ocin = [sb(st3, f"ocin{i}", [128, D], BF16) for i in range(2)]
        pzt = ps(st3, "pzt", [128, NH])
        P.dma("y", lambda h: h.dma_start(zr16[:], ZD), ["zd_d"], ["zr16_r"])
        P.op("v", lambda h: h.reciprocal(zr16[:], zr16[:]),
             ["zr16_r"], ["zr16"])
        for qc in range(NT):
            b, bk = ocin[qc % 2], f"ocin{qc % 2}"
            P.dma("y", (lambda qc, b: lambda h: h.dma_start(
                b[:], OTD[:, qc * 128:(qc + 1) * 128], transpose=True))(qc, b),
                rowkeys("outT_d", NH), [bk])
            P.op("t", (lambda qc: lambda h: h.transpose(
                pzt[:], zr16[:, qc * 128:(qc + 1) * 128], idmat[0:NH, 0:NH]))(qc),
                ["zr16", "idmat"], ["pzt"])
            P.op("v", lambda h: h.tensor_copy(zbc[:], pzt[:]),
                 ["pzt"], ["zbc"])
            P.op("v", (lambda qc: lambda h: h.tensor_copy(
                zb[:].rearrange("p (a b) -> p a b", b=128),
                zbc[:].unsqueeze(2).broadcast_to([128, NH, 128])))(qc),
                ["zbc"], ["zb"])
            P.op("v", (lambda b: lambda h: h.tensor_tensor(
                outm[:], b[:], zb[:], ALU.mult))(b), [bk, "zb"], ["outm"])
            rms_stats("outm", outm)
            quant_rows("outm", outm, "o", "A", qc, None)
        P.op("v", lambda h: h.tensor_scalar(
            SB["osc_o"][:], SB["osc_o"][:], SB["iw_o"][:, 0:1], 1.0 / 127.0,
            ALU.mult, ALU.mult), ["osc_o", "iw_o"], ["osc_o"])
        P.barrier()
        st3.close()

        st4 = ExitStack()
        projf = sb(st4, "projf", [128, D])
        s1 = wbuf[0]
        h1b = sb(st4, "h1b", [128, D], BF16)

        def o_epilogue(tc, ob, pi, pmm):
            P.op("s", (lambda tc, ob, pi: lambda h: h.activation(
                projf[:, ob * 512:(ob + 1) * 512], pmm[pi][:], AF.Copy,
                scale=SB["osc_o"][:, tc:tc + 1]))(tc, ob, pi),
                [f"pmm{pi}", "osc_o"], ["projf"])
            if ob == 3:
                b, bk = xb[tc % 2], f"xb{tc % 2}"
                P.dma("y", (lambda tc, b: lambda h: h.dma_start(
                    b[:], X[tc * 128:(tc + 1) * 128, :]))(tc, b), [], [bk])
                P.op("v", (lambda b: lambda h: h.tensor_tensor(
                    s1[:], projf[:], b[:], ALU.add))(b),
                    ["projf", bk], ["s1", "wbuf0"])
                P.op("v", lambda h: h.reduce_sum(SB["smean"][:], s1[:], AX.X),
                     ["s1"], ["smean"])
                P.op("s", lambda h: h.activation(scr[:], s1[:], AF.Square,
                                                 accum_out=SB["ssq2"][:]),
                     ["s1"], ["scr", "ssq2"])
                P.op("v", lambda h: h.tensor_scalar_mul(
                    SB["mean"][:], SB["smean"][:], 1.0 / D),
                    ["smean"], ["mean"])
                P.op("v", lambda h: h.tensor_scalar_mul(
                    SB["msq"][:], SB["ssq2"][:], 1.0 / D),
                    ["ssq2"], ["msq"])
                P.op("v", lambda h: h.tensor_tensor(
                    SB["mean2"][:], SB["mean"][:], SB["mean"][:], ALU.mult),
                    ["mean"], ["mean2"])
                P.op("v", lambda h: h.tensor_tensor(
                    SB["var"][:], SB["msq"][:], SB["mean2"][:], ALU.subtract),
                    ["msq", "mean2"], ["var"])
                P.op("s", lambda h: h.activation(SB["lrow"][:], SB["var"][:],
                                                 AF.Ln,
                                                 bias=SB["c_eps5"][:, 0:1]),
                     ["var", "c_eps5"], ["lrow"])
                P.op("s", lambda h: h.activation(SB["rstd"][:], SB["lrow"][:],
                                                 AF.Exp,
                                                 scale=SB["c_neg05"][:, 0:1]),
                     ["lrow", "c_neg05"], ["rstd"])
                P.op("v", lambda h: h.tensor_scalar(
                    h1b[:], s1[:], SB["mean"][:, 0:1], SB["rstd"][:, 0:1],
                    ALU.subtract, ALU.mult),
                    ["s1", "mean", "rstd"], ["h1b"])
                P.dma("y", (lambda tc: lambda h: h.dma_start(
                    H1[tc * 128:(tc + 1) * 128, :], h1b[:]))(tc),
                    ["h1b"], [f"h1_out:{tc}"])

        projection(st4, "o", None, "osc_o", epilogue=o_epilogue)
        P.barrier()
        st4.close()

        P.emit(stack)
    return nc


def l1_inputs(x_b, wq, wk, wv, wo, gq, gk, gv, go):
    return {
        "x": np.ascontiguousarray(x_b, np.float32),
        "wqT": np.ascontiguousarray(wq.T, np.float32),
        "wkT": np.ascontiguousarray(wk.T, np.float32),
        "wvT": np.ascontiguousarray(wv.T, np.float32),
        "woT": np.ascontiguousarray(wo.T, np.float32),
        "gq": np.ascontiguousarray(gq.reshape(1, D), np.float32),
        "gk": np.ascontiguousarray(gk.reshape(1, D), np.float32),
        "gv": np.ascontiguousarray(gv.reshape(1, D), np.float32),
        "go": np.ascontiguousarray(go.reshape(1, D), np.float32),
    }




TT_ALL = 8 * T
NCHUNK = TT_ALL // 128
HB = [512, 512, 512, 512, 512, 256]
NHC = HF // 128


def build_l2():
    nc = bass.Bass("TRN2", target_bir_lowering=False)
    H1F = nc.dram_tensor("h1f", [TT_ALL, D], BF16, kind="ExternalInput").ap()
    H1M = nc.dram_tensor("h1m", [T, D], BF16, kind="ExternalInput").ap()
    WG = nc.dram_tensor("wg", [D, NE], F32, kind="ExternalInput").ap()
    W1 = nc.dram_tensor("w1", [D, HF], F32, kind="ExternalInput").ap()
    W3 = nc.dram_tensor("w3", [D, HF], F32, kind="ExternalInput").ap()
    W2 = nc.dram_tensor("w2", [HF, D], F32, kind="ExternalInput").ap()
    SW1 = nc.dram_tensor("sw1", [D, HF], F32, kind="ExternalInput").ap()
    SW3 = nc.dram_tensor("sw3", [D, HF], F32, kind="ExternalInput").ap()
    SW2 = nc.dram_tensor("sw2", [HF, D], F32, kind="ExternalInput").ap()
    EID = nc.dram_tensor("eid", [1, 1], F32, kind="ExternalInput").ap()

    OE = nc.dram_tensor("oe", [CAP, D], BF16, kind="ExternalOutput").ap()
    PART = nc.dram_tensor("partial", [T, D], BF16, kind="ExternalOutput").ap()
    GK = nc.dram_tensor("gk", [1, TT_ALL], F32, kind="ExternalOutput").ap()
    SLOT = nc.dram_tensor("slot", [1, TT_ALL], F32, kind="ExternalOutput").ap()
    DBG = nc.dram_tensor("dbg", [4, TT_ALL], F32, kind="ExternalOutput").ap()

    S1D = nc.dram_tensor("s1d", [T, HF], BF16, kind="Internal").ap()
    GD = nc.dram_tensor("gd", [T, HF], BF16, kind="Internal").ap()
    YSH = nc.dram_tensor("ysh", [T, D], BF16, kind="Internal").ap()
    XD = nc.dram_tensor("xd", [CAP + 16, D], BF16, kind="Internal").ap()
    LGTD = nc.dram_tensor("lgtd", [NE, TT_ALL], F32, kind="Internal").ap()
    BND = nc.dram_tensor("bnd", [64, 1024], F32, kind="Internal").ap()
    ECOL = nc.dram_tensor("ecol", [NE, 16], F32, kind="Internal").ap()
    TRID = nc.dram_tensor("trid", [16, 16], F32, kind="Internal").ap()
    IDXD = nc.dram_tensor("idxd", [1, TT_ALL], F32, kind="Internal").ap()

    stack = ExitStack()
    with stack:
        SB = {}
        P = Prog(nc)
        uid = [0]

        def sb(st, name, shape, dt=F32):
            uid[0] += 1
            t = st.enter_context(nc.sbuf_tensor(f"{name}_{uid[0]}", shape, dt))
            SB[name] = t
            return t

        def ps(st, name, shape, dt=F32):
            uid[0] += 1
            return st.enter_context(
                nc.psum_tensor(f"{name}_{uid[0]}", shape, dt))

        ones_row8 = sb(stack, "ones_row8", [1, NE])
        iorow = sb(stack, "iorow", [128, 128])
        iocol = sb(stack, "iocol", [128, 128])
        trif = sb(stack, "trif", [128, 128])
        idm = sb(stack, "idm", [128, 128])
        zt = sb(stack, "zt", [128, D], BF16)
        eidt = sb(stack, "eidt", [1, 1])
        iota8 = sb(stack, "iota8", [NE, 16])
        P.op("v", lambda h: h.memset(ones_row8[:], 1.0), [], ["ones_row8"])
        P.op("v", lambda h: h.memset(zt[:], 0.0), [], ["zt"])
        P.op("g", lambda h: h.iota(iorow[:], [[1, 128]], base=0,
                                   channel_multiplier=0,
                                   allow_small_or_imprecise_dtypes=True),
             [], ["iorow"])
        P.op("g", lambda h: h.iota(iocol[:], [[0, 128]], base=0,
                                   channel_multiplier=1,
                                   allow_small_or_imprecise_dtypes=True),
             [], ["iocol"])
        P.op("v", lambda h: h.tensor_tensor(trif[:], iocol[:], iorow[:],
                                            ALU.is_lt),
             ["iorow", "iocol"], ["trif"])
        P.op("v", lambda h: h.tensor_tensor(idm[:], iocol[:], iorow[:],
                                            ALU.is_equal),
             ["iorow", "iocol"], ["idm"])
        P.op("g", lambda h: h.iota(iota8[:], [[0, 16]], base=0,
                                   channel_multiplier=1,
                                   allow_small_or_imprecise_dtypes=True),
             [], ["iota8"])
        P.dma("y", lambda h: h.dma_start(eidt[:], EID), [], ["eidt"])
        P.dma("g", lambda h: h.dma_start(ECOL, iota8[:]), ["iota8"], ["ecold"])
        P.dma("g", lambda h: h.dma_start(TRID, trif[0:16, 0:16]),
             ["trif"], ["trid_d"])

        def ffn(tag, XT_dram, xt_rows, W1d, W3d, W2d, DST, dstkeys,
                whalves=1, w2div=None):
            if w2div is None:
                w2div = whalves
            stx = ExitStack()
            xT = [sb(stx, f"xT{tag}{i}", [128, xt_rows], BF16)
                  for i in range(NT)]
            hwid = HF // whalves
            wres = [sb(stx, f"wres{tag}{i}", [128, hwid], BF16)
                    for i in range(NT)]
            sgb = [sb(stx, f"sgb{tag}{i}", [128, 512], BF16) for i in range(2)]
            s1b = [sb(stx, f"s1b{tag}{i}", [128, 512], BF16) for i in range(2)]
            pfm = [ps(stx, f"pfm{tag}{i}", [128, 512]) for i in range(2)]
            ntc = xt_rows // 128
            if whalves == 1:
                HBL = [(0, HB)]
            else:
                HBL = [(0, [512, 512, 384]), (1408, [512, 512, 384])]
            for cc in range(NT):
                P.dma("y", (lambda cc: lambda h: h.dma_start(
                    xT[cc][:], XT_dram[0:xt_rows, cc * 128:(cc + 1) * 128],
                    transpose=True))(cc),
                    dstkeys, [f"xT{tag}{cc}"])
            for phase, WD in (("a", W1d), ("b", W3d)):
                for hoff, HBs in HBL:
                    for cc in range(NT):
                        P.dma("g", (lambda cc, WD, hoff: lambda h: h.dma_start(
                            wres[cc][:],
                            WD[cc * 128:(cc + 1) * 128,
                               hoff:hoff + hwid]))(cc, WD, hoff),
                            [f"wres{tag}{cc}"], [f"wres{tag}{cc}"])
                    for tc in range(ntc):
                        off = 0
                        for hbw in HBs:
                            pi = (tc * len(HBs) + (off // 512)) % 2
                            goff = hoff + off
                            for cc in range(NT):
                                P.op("t", (lambda tc, off, hbw, cc, pi:
                                           lambda h: h.matmul(
                                    pfm[pi][:, 0:hbw],
                                    xT[cc][:, tc * 128:(tc + 1) * 128],
                                    wres[cc][:, off:off + hbw],
                                    start=(cc == 0), stop=(cc == NT - 1)))(
                                        tc, off, hbw, cc, pi),
                                    [f"xT{tag}{cc}", f"wres{tag}{cc}"],
                                    [f"pfm{tag}{pi}"])
                            if phase == "a":
                                eb, ek = sgb[pi], f"sgb{tag}{pi}"
                                P.op("s", (lambda pi, hbw, eb: lambda h:
                                           h.activation(
                                    eb[:, 0:hbw], pfm[pi][:, 0:hbw],
                                    AF.Silu))(pi, hbw, eb),
                                    [f"pfm{tag}{pi}"], [ek])
                                P.dma("y", (lambda tc, goff, hbw, eb:
                                            lambda h: h.dma_start(
                                    S1D[tc * 128:(tc + 1) * 128,
                                        goff:goff + hbw],
                                    eb[:, 0:hbw]))(tc, goff, hbw, eb),
                                    [ek], [f"s1d{tag}:{tc}"])
                            else:
                                lb, lk = s1b[pi], f"s1b{tag}{pi}"
                                P.dma("y", (lambda tc, goff, hbw, lb:
                                            lambda h: h.dma_start(
                                    lb[:, 0:hbw],
                                    S1D[tc * 128:(tc + 1) * 128,
                                        goff:goff + hbw]))(tc, goff, hbw, lb),
                                    [f"s1d{tag}:{tc}"], [lk])
                                eb, ek = sgb[pi], f"sgb{tag}{pi}"
                                P.op("v", (lambda pi, hbw, lb, eb: lambda h:
                                           h.tensor_tensor(
                                    eb[:, 0:hbw], lb[:, 0:hbw],
                                    pfm[pi][:, 0:hbw], ALU.mult))(
                                        pi, hbw, lb, eb),
                                    [f"pfm{tag}{pi}", lk], [ek])
                                P.dma("y", (lambda tc, goff, hbw, eb:
                                            lambda h: h.dma_start(
                                    GD[tc * 128:(tc + 1) * 128,
                                       goff:goff + hbw],
                                    eb[:, 0:hbw]))(tc, goff, hbw, eb),
                                    [ek], [f"gd{tag}:{tc}"])
                            off += hbw
            P.barrier()
            stx.close()
            sty = ExitStack()
            gT = [sb(sty, f"gT{tag}{i}", [128, xt_rows], BF16)
                  for i in range(NHC)]
            dwid = D // w2div
            w2r = [sb(sty, f"w2r{tag}{i}", [128, dwid], BF16)
                   for i in range(NHC)]
            oeb = [sb(sty, f"oeb{tag}{i}", [128, 512], BF16) for i in range(2)]
            pfc = [ps(sty, f"pfc{tag}{i}", [128, 512]) for i in range(2)]
            for hc in range(NHC):
                P.dma("y", (lambda hc: lambda h: h.dma_start(
                    gT[hc][:], GD[0:xt_rows, hc * 128:(hc + 1) * 128],
                    transpose=True))(hc),
                    rowkeys(f"gd{tag}", ntc), [f"gT{tag}{hc}"])
            for dhalf in range(w2div):
                doff = dhalf * dwid
                for hc in range(NHC):
                    P.dma("g", (lambda hc, doff: lambda h: h.dma_start(
                        w2r[hc][:],
                        W2d[hc * 128:(hc + 1) * 128,
                            doff:doff + dwid]))(hc, doff),
                        [f"w2r{tag}{hc}"], [f"w2r{tag}{hc}"])
                for tc in range(ntc):
                    for ob in range(dwid // 512):
                        pi = (tc * (dwid // 512) + ob) % 2
                        for hc in range(NHC):
                            P.op("t", (lambda tc, ob, hc, pi: lambda h:
                                       h.matmul(
                                pfc[pi][:],
                                gT[hc][:, tc * 128:(tc + 1) * 128],
                                w2r[hc][:, ob * 512:(ob + 1) * 512],
                                start=(hc == 0), stop=(hc == NHC - 1)))(
                                    tc, ob, hc, pi),
                                [f"gT{tag}{hc}", f"w2r{tag}{hc}"],
                                [f"pfc{tag}{pi}"])
                        eb, ek = oeb[pi], f"oeb{tag}{pi}"
                        P.op("s", (lambda pi, eb: lambda h: h.activation(
                            eb[:], pfc[pi][:], AF.Copy))(pi, eb),
                            [f"pfc{tag}{pi}"], [ek])
                        P.dma("y", (lambda tc, ob, doff, eb: lambda h:
                                    h.dma_start(
                            DST[tc * 128:(tc + 1) * 128,
                                doff + ob * 512:doff + (ob + 1) * 512],
                            eb[:]))(tc, ob, doff, eb),
                            [ek], [f"ffdst{tag}:{tc}"])
            P.barrier()
            sty.close()

        import os as _os
        _SKIPA = _os.environ.get("SKIPA")
        _SKIPB = _os.environ.get("SKIPB")
        stA = ExitStack()
        h1t = [sb(stA, f"h1t{i}", [128, 1024], BF16) for i in range(NT)]
        wgt = [sb(stA, f"wgt{i}", [128, NE], BF16) for i in range(NT)]
        lgb = [sb(stA, f"lgb{i}", [NE, 512]) for i in range(2)]
        plg = ps(stA, "plg", [NE, 512])
        for cc in range(NT):
            P.dma("g", (lambda cc: lambda h: h.dma_start(
                wgt[cc][:], WG[cc * 128:(cc + 1) * 128, :]))(cc),
                [], [f"wgt{cc}"])
        for sq in range(2 * NE):
            for cc in range(NT):
                P.dma("y", (lambda sq, cc: lambda h: h.dma_start(
                    h1t[cc][:],
                    H1F[sq * 1024:(sq + 1) * 1024,
                        cc * 128:(cc + 1) * 128],
                    transpose=True))(sq, cc),
                    [], [f"h1t{cc}"])
            for jb in range(2):
                for cc in range(NT):
                    P.op("t", (lambda sq, jb, cc: lambda h: h.matmul(
                        plg[:], wgt[cc][:],
                        h1t[cc][:, jb * 512:(jb + 1) * 512],
                        start=(cc == 0), stop=(cc == NT - 1)))(sq, jb, cc),
                        [f"wgt{cc}", f"h1t{cc}"], ["plg"])
                lb, lk = lgb[jb % 2], f"lgb{jb % 2}"
                P.op("v", (lambda lb: lambda h: h.tensor_copy(
                    lb[:], plg[:]))(lb), ["plg"], [lk])
                P.dma("y", (lambda sq, jb, lb: lambda h: h.dma_start(
                    LGTD[:, sq * 1024 + jb * 512: sq * 1024 + (jb + 1) * 512],
                    lb[:]))(sq, jb, lb), [lk], ["lgtd"])
        P.barrier()
        stA.close()

        stB = ExitStack()
        lgP = sb(stB, "lgP", [128, 1024])
        mbP = sb(stB, "mbP", [128, 1024])
        ohP = sb(stB, "ohP", [128, 1024])
        exP = sb(stB, "exP", [128, 1024])
        cumA = sb(stB, "cumA", [128, 1024])
        prodP = exP
        cumB = mbP
        f64 = mbP.ap() if False else mbP
        fsc = sb(stB, "fsc", [64, 1024])
        m16 = sb(stB, "m16", [16, 1024])
        mxs = sb(stB, "mxs", [16, 1024])
        se16 = sb(stB, "se16", [16, 1024])
        e16 = sb(stB, "e16", [16, 1024])
        pos16 = sb(stB, "pos16", [16, 1024])
        keep16 = sb(stB, "keep16", [16, 1024])
        gate16 = sb(stB, "gate16", [16, 1024])
        gk16 = sb(stB, "gk16", [16, 1024])
        slot16 = sb(stB, "slot16", [16, 1024])
        val16 = sb(stB, "val16", [16, 1024])
        idx16 = sb(stB, "idx16", [16, 1024])
        ecol = sb(stB, "ecol_s", [128, 1])
        eidc = sb(stB, "eidc", [16, 1])
        tot = sb(stB, "tot", [128, 1])
        offs = sb(stB, "offs", [128, 1])
        mblk = sb(stB, "mblk", [128, 128])
        idxm = sb(stB, "idxm", [128, 128])
        idxmT = sb(stB, "idxmT", [128, 128])
        idxi = sb(stB, "idxi", [128, 128], I32)
        poffs = ps(stB, "poffs", [128, 1])
        ptr = ps(stB, "ptr", [128, 128])

        P.dma("g", lambda h: h.dma_start(
            ecol[:], ECOL.rearrange("e s -> (e s)").unsqueeze(1)),
            ["ecold"], ["ecol_s"])
        P.dma("g", lambda h: h.dma_start(
            eidc[:], EID.broadcast_to([16, 1])),
            ["eidt"], ["eidc"])
        P.op("v", lambda h: h.memset(mblk[:], 0.0), [], ["mblk_z"])
        for e in range(NE):
            P.dma("g", (lambda e: lambda h: h.dma_start(
                mblk[e * 16:(e + 1) * 16, e * 16:(e + 1) * 16], TRID))(e),
                ["trid_d", "mblk_z"], ["mblk"])
        P.dma("g", lambda h: h.dma_start(
            lgP[:], LGTD.rearrange("e (s i) -> (e s) i", i=1024)),
            ["lgtd"], ["lgP"])

        def fold(op, src, srck, dstk_pref):
            P.dma("g", (lambda: lambda h: h.dma_start(
                BND, src[64:128, :]))(), [srck], ["bndd"])
            P.dma("g", (lambda: lambda h: h.dma_start(
                fsc[:], BND))(), ["bndd"], ["fsc"])
            P.op("v", (lambda: lambda h: h.tensor_tensor(
                f64[0:64, :], src[0:64, :], fsc[:], op))(),
                [srck, "fsc"], [dstk_pref + "64", "mbP"])
            P.dma("g", (lambda: lambda h: h.dma_start(
                BND[0:32, :], f64[32:64, :]))(),
                [dstk_pref + "64", "mbP"], ["bndd"])
            P.dma("g", (lambda: lambda h: h.dma_start(
                fsc[0:32, :], BND[0:32, :]))(), ["bndd"], ["fsc"])
            P.op("v", (lambda: lambda h: h.tensor_tensor(
                f64[0:32, :], f64[0:32, :], fsc[0:32, :], op))(),
                [dstk_pref + "64", "fsc", "mbP"], [dstk_pref + "32", "mbP"])
            P.dma("g", (lambda: lambda h: h.dma_start(
                BND[0:16, :], f64[16:32, :]))(),
                [dstk_pref + "32", "mbP"], ["bndd"])
            P.dma("g", (lambda: lambda h: h.dma_start(
                fsc[0:16, :], BND[0:16, :]))(), ["bndd"], ["fsc"])
            P.op("v", (lambda: lambda h: h.tensor_tensor(
                m16[:], f64[0:16, :], fsc[0:16, :], op))(),
                [dstk_pref + "32", "fsc", "mbP"], [dstk_pref + "16"])

        fold(ALU.max, lgP, "lgP", "mx")
        P.op("v", lambda h: h.tensor_copy(mxs[:], m16[:]),
             ["mx16"], ["m16f"])
        P.dma("g", lambda h: h.dma_start(BND[0:16, :], mxs[:]), ["m16f"], ["bnd2"])
        for e in range(NE):
            P.dma("g", (lambda e: lambda h: h.dma_start(
                mbP[e * 16:(e + 1) * 16, :], BND[0:16, :]))(e),
                ["bnd2"], ["mbP"])
        P.op("v", lambda h: h.tensor_tensor(ohP[:], lgP[:], mbP[:],
                                            ALU.is_ge),
             ["lgP", "mbP"], ["ohP"])
        P.op("s", lambda h: h.activation(exP[:], lgP[:], AF.Exp),
             ["lgP"], ["exP"])
        fold(ALU.add, exP, "exP", "sx")
        P.op("v", lambda h: h.tensor_copy(se16[:], m16[:]),
             ["sx16"], ["se16"])
        P.op("v", lambda h: h.tensor_scalar_mul(prodP[:], ohP[:],
                                                ecol[:, 0:1]),
             ["ohP", "ecol_s"], ["prodP", "exP"])
        fold(ALU.add, prodP, "prodP", "ex")
        P.op("v", lambda h: h.tensor_copy(e16[:], m16[:]),
             ["ex16"], ["e16"])
        P.op("v", lambda h: h.tensor_copy(cumA[:], ohP[:]), ["ohP"], ["cumA"])
        cur, curk, oth, othk = cumA, "cumA", cumB, "cumB"
        P.op("v", lambda h: h.tensor_copy(cumB[:, 0:1], cumA[:, 0:1]),
             ["cumA", "mbP"], ["cumB", "mbP"])
        shift = 1
        while shift < 1024:
            P.op("v", (lambda cur, oth, shift: lambda h: h.tensor_copy(
                oth[:, 0:shift], cur[:, 0:shift]))(cur, oth, shift),
                [curk], [othk])
            P.op("v", (lambda cur, oth, shift: lambda h: h.tensor_tensor(
                oth[:, shift:], cur[:, shift:], cur[:, 0:1024 - shift],
                ALU.add))(cur, oth, shift),
                [curk], [othk])
            cur, curk, oth, othk = oth, othk, cur, curk
            shift *= 2
        P.op("v", (lambda cur: lambda h: h.tensor_copy(
            tot[:], cur[:, 1023:1024]))(cur), [curk], ["tot"])
        P.op("t", lambda h: h.matmul(poffs[:], mblk[:], tot[:],
                                     start=True, stop=True),
             ["mblk", "tot"], ["poffs"])
        P.op("v", lambda h: h.tensor_copy(offs[:], poffs[:]),
             ["poffs"], ["offs"])
        P.op("v", (lambda cur: lambda h: h.tensor_scalar_add(
            cur[:], cur[:], offs[:, 0:1]))(cur),
            [curk, "offs"], [curk + "g"])
        P.op("v", (lambda cur: lambda h: h.tensor_tensor(
            prodP[:], ohP[:], cur[:], ALU.mult))(cur),
            ["ohP", curk + "g"], ["prodP2", "exP"])
        fold(ALU.add, prodP, "prodP2", "ps")
        P.op("v", lambda h: h.tensor_copy(pos16[:], m16[:]),
             ["ps16"], ["pos16"])
        P.op("v", lambda h: h.tensor_single_scalar(
            keep16[:], pos16[:], CAP + 0.5, ALU.is_lt),
            ["pos16"], ["keep16"])
        P.op("s", lambda h: h.activation(gate16[:], mxs[:], AF.Exp),
             ["m16f"], ["gate_e"])
        P.op("v", lambda h: h.reciprocal(val16[:], se16[:]),
             ["se16"], ["rse"])
        P.op("v", lambda h: h.tensor_tensor(gate16[:], gate16[:],
                                            val16[:], ALU.mult),
             ["gate_e", "rse"], ["gate16"])
        P.op("v", lambda h: h.tensor_tensor(gk16[:], gate16[:], keep16[:],
                                            ALU.mult),
             ["gate16", "keep16"], ["gk16"])
        P.op("v", lambda h: h.scalar_tensor_tensor(
            slot16[:], e16[:], float(CAP), pos16[:], ALU.mult, ALU.add),
            ["e16", "pos16"], ["slot_a"])
        P.op("v", lambda h: h.tensor_scalar(
            slot16[:], slot16[:], -1.0, float(TT_ALL - 1),
            ALU.add, ALU.min), ["slot_a"], ["slot16"])
        P.op("v", lambda h: h.tensor_single_scalar(
            val16[:], e16[:], eidc[:, 0:1], ALU.is_equal),
            ["e16", "eidc"], ["val_a"])
        P.op("v", lambda h: h.tensor_tensor(val16[:], val16[:], keep16[:],
                                            ALU.mult),
             ["val_a", "keep16"], ["val16"])
        P.op("v", lambda h: h.tensor_single_scalar(
            idx16[:], pos16[:], -1.0 - CAP, ALU.add),
            ["pos16"], ["idx_a"])
        P.op("v", lambda h: h.tensor_tensor(idx16[:], idx16[:], val16[:],
                                            ALU.mult),
             ["idx_a", "val16"], ["idx_b"])
        P.op("v", lambda h: h.tensor_single_scalar(
            idx16[:], idx16[:], float(CAP), ALU.add),
            ["idx_b"], ["idx16"])
        P.dma("y", lambda h: h.dma_start(
            GK.rearrange("o (s i) -> (o s) i", i=1024), gk16[:]),
            ["gk16"], ["gk_o"])
        P.dma("y", lambda h: h.dma_start(
            DBG[0:1, :].rearrange("o (s i) -> (o s) i", i=1024), mxs[:]),
            ["m16f"], ["dbg_o"])
        P.dma("y", lambda h: h.dma_start(
            DBG[1:2, :].rearrange("o (s i) -> (o s) i", i=1024), e16[:]),
            ["e16"], ["dbg_o"])
        P.dma("y", lambda h: h.dma_start(
            DBG[2:3, :].rearrange("o (s i) -> (o s) i", i=1024), pos16[:]),
            ["pos16"], ["dbg_o"])
        P.dma("y", lambda h: h.dma_start(
            DBG[3:4, 0:128].rearrange("o (s i) -> (o s) i", i=1),
            tot[:]), ["tot"], ["dbg_o"])
        P.dma("y", lambda h: h.dma_start(
            DBG[3:4, 128:256].rearrange("o (s i) -> (o s) i", i=1),
            offs[:]), ["offs"], ["dbg_o"])
        P.dma("y", lambda h: h.dma_start(
            SLOT.rearrange("o (s i) -> (o s) i", i=1024), slot16[:]),
            ["slot16"], ["slot_o"])
        P.dma("g", lambda h: h.dma_start(
            IDXD.rearrange("o (s i) -> (o s) i", i=1024), idx16[:]),
            ["idx16"], ["idxd"])
        P.dma("g", lambda h: h.dma_start(
            idxm[:], IDXD.rearrange("o (p j) -> (o p) j", p=128)),
            ["idxd"], ["idxm"])
        P.op("t", lambda h: h.transpose(ptr[:], idxm[:], idm[:]),
             ["idxm", "idm"], ["ptr"])
        P.op("v", lambda h: h.tensor_copy(idxmT[:], ptr[:]), ["ptr"], ["idxmT"])
        P.op("v", lambda h: h.tensor_copy(idxi[:], idxmT[:]),
             ["idxmT"], ["idxi"])

        for tc in range(NT):
            P.dma("y", (lambda tc: lambda h: h.dma_start(
                XD[tc * 128:(tc + 1) * 128, :], zt[:]))(tc),
                ["zt"], [f"xdz:{tc}"])
        hch = [sb(stB, f"hch{i}", [128, D], BF16) for i in range(2)]
        for j in range(NCHUNK):
            b, bk = hch[j % 2], f"hch{j % 2}"
            P.dma("g", (lambda j, b: lambda h: h.dma_start(
                b[:], H1F[j * 128:(j + 1) * 128, :]))(j, b), [], [bk])
            P.dma("g", (lambda j, b: lambda h: h.indirect_dma_start(
                XD, bass.IndirectOffsetOnAxis(idxi[:, j:j + 1], 0),
                b[:], None))(j, b),
                [bk, "idxi"] + rowkeys("xdz"), ["xd_d"])

        import os as _os2
        if not _os2.environ.get("OVERLAP_S"):
            P.barrier()
            stB.close()
            ffn("S", H1M, T, SW1, SW3, SW2, YSH, [], whalves=2, w2div=4)
        else:
            ffn("S", H1M, T, SW1, SW3, SW2, YSH, [], whalves=2, w2div=4)
            P.barrier()
            stB.close()

        stP = ExitStack()
        pa = sb(stP, "pa", [128, D], BF16)
        pb = sb(stP, "pb", [128, D], BF16)
        pc_ = sb(stP, "pc_", [128, D], BF16)
        for tc in range(NT):
            P.dma("y", (lambda tc: lambda h: h.dma_start(
                pa[:], H1M[tc * 128:(tc + 1) * 128, :]))(tc), ["pa_r"], ["pa"])
            P.dma("y", (lambda tc: lambda h: h.dma_start(
                pb[:], YSH[tc * 128:(tc + 1) * 128, :]))(tc),
                [f"ffdstS:{tc}", "pb_r"], ["pb"])
            P.op("v", lambda h: h.tensor_tensor(pc_[:], pa[:], pb[:],
                                                ALU.add),
                 ["pa", "pb"], ["pc_", "pa_r", "pb_r"])
            P.dma("y", (lambda tc: lambda h: h.dma_start(
                PART[tc * 128:(tc + 1) * 128, :], pc_[:]))(tc),
                ["pc_"], ["part_o"])
        P.barrier()
        stP.close()

        ffn("E", XD, CAP, W1, W3, W2, OE, ["xd_d"], whalves=1)

        P.emit(stack)
    return nc




def build_l3():
    nc = bass.Bass("TRN2", target_bir_lowering=False)
    OEF = nc.dram_tensor("oef", [TT_ALL, D], BF16, kind="ExternalInput").ap()
    PART = nc.dram_tensor("partial", [T, D], BF16, kind="ExternalInput").ap()
    GKM = nc.dram_tensor("gkm", [128, NT], F32, kind="ExternalInput").ap()
    SLM = nc.dram_tensor("slm", [128, NT], F32, kind="ExternalInput").ap()
    Y = nc.dram_tensor("y", [T, D], F32, kind="ExternalOutput").ap()

    stack = ExitStack()
    with stack:
        SB = {}
        P = Prog(nc)

        def sb(name, shape, dt=F32):
            t = stack.enter_context(nc.sbuf_tensor(name, shape, dt))
            SB[name] = t
            return t

        gkm = sb("gkm_s", [128, NT])
        slm = sb("slm_s", [128, NT])
        sli = sb("sli", [128, NT], I32)
        moe = sb("moe", [128, D], BF16)
        par = sb("par", [128, D], BF16)
        s2 = sb("s2", [128, D])
        scr = sb("scr", [128, D])
        yb = sb("yb", [128, D])
        c_eps5 = sb("c_eps5", [128, 1])
        c_neg05 = sb("c_neg05", [128, 1])
        for nm, val in (("c_eps5", 1e-5), ("c_neg05", -0.5)):
            P.op("v", (lambda nm, val: lambda h: h.memset(SB[nm + "_s"]
                 if nm + "_s" in SB else SB[nm][:], val))(nm, val), [], [nm])
        for nm in ("ssq2", "smean", "msq", "mean", "mean2", "var", "lrow",
                   "rstd"):
            sb(nm, [128, 1])
        P.dma("y", lambda h: h.dma_start(gkm[:], GKM), [], ["gkm"])
        P.dma("y", lambda h: h.dma_start(slm[:], SLM), [], ["slm"])
        P.op("v", lambda h: h.tensor_copy(sli[:], slm[:]), ["slm"], ["sli"])
        for tc in range(NT):
            P.dma("g", (lambda tc: lambda h: h.indirect_dma_start(
                moe[:], None, OEF,
                bass.IndirectOffsetOnAxis(sli[:, tc:tc + 1], 0)))(tc),
                ["sli", "moe_r"], ["moe"])
            P.dma("y", (lambda tc: lambda h: h.dma_start(
                par[:], PART[tc * 128:(tc + 1) * 128, :]))(tc), [], ["par"])
            P.op("v", (lambda tc: lambda h: h.scalar_tensor_tensor(
                s2[:], moe[:], gkm[:, tc:tc + 1], par[:],
                ALU.mult, ALU.add))(tc),
                ["moe", "gkm", "par"], ["s2", "moe_r"])
            P.op("v", lambda h: h.reduce_sum(SB["smean"][:], s2[:], AX.X),
                 ["s2"], ["smean"])
            P.op("s", lambda h: h.activation(scr[:], s2[:], AF.Square,
                                             accum_out=SB["ssq2"][:]),
                 ["s2"], ["scr", "ssq2"])
            P.op("v", lambda h: h.tensor_scalar_mul(SB["mean"][:],
                                                    SB["smean"][:], 1.0 / D),
                 ["smean"], ["mean"])
            P.op("v", lambda h: h.tensor_scalar_mul(SB["msq"][:],
                                                    SB["ssq2"][:], 1.0 / D),
                 ["ssq2"], ["msq"])
            P.op("v", lambda h: h.tensor_tensor(SB["mean2"][:], SB["mean"][:],
                                                SB["mean"][:], ALU.mult),
                 ["mean"], ["mean2"])
            P.op("v", lambda h: h.tensor_tensor(SB["var"][:], SB["msq"][:],
                                                SB["mean2"][:], ALU.subtract),
                 ["msq", "mean2"], ["var"])
            P.op("s", lambda h: h.activation(SB["lrow"][:], SB["var"][:],
                                             AF.Ln,
                                             bias=SB["c_eps5"][:, 0:1]),
                 ["var", "c_eps5"], ["lrow"])
            P.op("s", lambda h: h.activation(SB["rstd"][:], SB["lrow"][:],
                                             AF.Exp,
                                             scale=SB["c_neg05"][:, 0:1]),
                 ["lrow", "c_neg05"], ["rstd"])
            P.op("v", lambda h: h.tensor_scalar(
                yb[:], s2[:], SB["mean"][:, 0:1], SB["rstd"][:, 0:1],
                ALU.subtract, ALU.mult),
                ["s2", "mean", "rstd"], ["yb"])
            P.dma("y", (lambda tc: lambda h: h.dma_start(
                Y[tc * 128:(tc + 1) * 128, :], yb[:]))(tc),
                ["yb"], ["y_o"])
        P.emit(stack)
    return nc



_NC_CACHE = {}


def _get(name, builder):
    if name not in _NC_CACHE:
        _NC_CACHE[name] = builder()
    return _NC_CACHE[name]


def kernel(L0_attn_x, L0_attn_wq, L0_attn_wk, L0_attn_wv, L0_attn_wo,
           L0_attn_gq, L0_attn_gk, L0_attn_gv, L0_attn_go,
           L0_ffn_Wg, L0_ffn_W1, L0_ffn_W3, L0_ffn_W2,
           L0_ffn_sw1, L0_ffn_sw3, L0_ffn_sw2):
    x = np.asarray(L0_attn_x)
    cores = list(range(8))

    nc1 = _get("l1", build_l1)
    in1 = [l1_inputs(x[b], L0_attn_wq, L0_attn_wk, L0_attn_wv, L0_attn_wo,
                     L0_attn_gq, L0_attn_gk, L0_attn_gv, L0_attn_go)
           for b in range(8)]
    r1 = run_bass_kernel_spmd(nc1, in1, core_ids=cores)
    h1f = np.concatenate([np.asarray(r1.results[b]["h1"]) for b in range(8)],
                         axis=0)

    nc2 = _get("l2", build_l2)
    in2 = []
    for e in range(8):
        in2.append({
            "h1f": h1f,
            "h1m": np.ascontiguousarray(h1f[e * T:(e + 1) * T]),
            "wg": np.ascontiguousarray(L0_ffn_Wg, np.float32),
            "w1": np.ascontiguousarray(L0_ffn_W1[e], np.float32),
            "w3": np.ascontiguousarray(L0_ffn_W3[e], np.float32),
            "w2": np.ascontiguousarray(L0_ffn_W2[e], np.float32),
            "sw1": np.ascontiguousarray(L0_ffn_sw1, np.float32),
            "sw3": np.ascontiguousarray(L0_ffn_sw3, np.float32),
            "sw2": np.ascontiguousarray(L0_ffn_sw2, np.float32),
            "eid": np.full((1, 1), float(e), np.float32),
        })
    r2 = run_bass_kernel_spmd(nc2, in2, core_ids=cores)
    oef = np.concatenate([np.asarray(r2.results[e]["oe"]) for e in range(8)],
                         axis=0)
    gk = np.asarray(r2.results[0]["gk"]).reshape(TT_ALL)
    slot = np.asarray(r2.results[0]["slot"]).reshape(TT_ALL)

    nc3 = _get("l3", build_l3)
    in3 = []
    for b in range(8):
        gkb = gk[b * T:(b + 1) * T].reshape(NT, 128).T
        slb = slot[b * T:(b + 1) * T].reshape(NT, 128).T
        in3.append({
            "oef": oef,
            "partial": np.asarray(r2.results[b]["partial"]),
            "gkm": np.ascontiguousarray(gkb, np.float32),
            "slm": np.ascontiguousarray(slb, np.float32),
        })
    r3 = run_bass_kernel_spmd(nc3, in3, core_ids=cores)
    y = np.stack([np.asarray(r3.results[b]["y"]) for b in range(8)], axis=0)
    return y


# version 04
# speedup vs baseline: 1.0044x; 1.0044x over previous
import math
import numpy as np
from contextlib import ExitStack

import concourse.bass as bass
import concourse.mybir as mybir
from concourse.bass_utils import run_bass_kernel_spmd

F32 = mybir.dt.float32
BF16 = mybir.dt.bfloat16
I16 = mybir.dt.int16
I32 = mybir.dt.int32
AF = mybir.ActivationFunctionType
ALU = mybir.AluOpType
AX = mybir.AxisListType

D = 2048
T = 2048
NH = 16
DK = 128
HF = 2816
NE = 8
CAP = 2048
NT = D // 128
SM_SCALE = 1.0 / math.sqrt(DK)
EXP_BIAS = -12.0

ENGINES = ("t", "v", "s", "g", "y")


class Prog:

    def __init__(self, nc):
        self.nc = nc
        self.ops = {e: [] for e in ENGINES}
        self.last_writer = {}
        self.readers = {}
        self.cidx = {e: 0 for e in ENGINES}
        self.didx = {e: 0 for e in ENGINES}
        self.referenced = {e: set() for e in ENGINES}
        self.vclock = {e: {} for e in ENGINES}
        self.last_tok = {e: None for e in ENGINES}
        self.pending_barrier = []

    def _dep(self, waits, ceng, tok):
        peng, kind, idx = tok
        if peng == ceng and kind == "c" and ceng == "t":
            return
        vc = self.vclock[ceng]
        key = (peng, kind)
        if vc.get(key, -1) >= idx:
            return
        vc[key] = idx
        waits.append(tok)
        if kind == "c":
            self.referenced[peng].add(idx)

    def _record(self, eng, kind, fn, reads, writes):
        waits = []
        for t in self.pending_barrier:
            if t is not None:
                self._dep(waits, eng, t)
        for k in reads:
            t = self.last_writer.get(k)
            if t is not None:
                self._dep(waits, eng, t)
        for k in writes:
            t = self.last_writer.get(k)
            if t is not None:
                self._dep(waits, eng, t)
            for t2 in self.readers.get(k, ()):
                self._dep(waits, eng, t2)
        idx = self.cidx[eng] if kind == "c" else self.didx[eng]
        tok = (eng, kind, idx)
        for k in reads:
            self.readers.setdefault(k, []).append(tok)
        for k in writes:
            self.last_writer[k] = tok
            self.readers[k] = []
        self.ops[eng].append((fn, kind, waits))
        self.last_tok[eng] = tok
        if kind == "c":
            self.cidx[eng] += 1
        else:
            self.didx[eng] += 1
        return tok

    def op(self, eng, fn, reads=(), writes=()):
        return self._record(eng, "c", fn, list(reads), list(writes))

    def dma(self, eng, fn, reads=(), writes=()):
        return self._record(eng, "d", fn, list(reads), list(writes))

    def barrier(self):
        self.pending_barrier = [self.last_tok[e] for e in ENGINES]

    NDL = 8

    def emit(self, stack):
        nc = self.nc
        csem = {e: stack.enter_context(nc.semaphore(f"c_{e}")) for e in ENGINES}
        dsem = {e: [stack.enter_context(nc.semaphore(f"d_{e}{i}"))
                    for i in range(self.NDL)]
                for e in ENGINES if self.didx[e] > 0}
        cval = {}
        for e in ENGINES:
            vals, v, ci = {}, 0, 0
            for (fn, kind, _) in self.ops[e]:
                if kind != "c":
                    continue
                if ci in self.referenced[e]:
                    v += 1
                vals[ci] = v
                ci += 1
            cval[e] = vals
        block = stack.enter_context(nc.Block())

        def make_section(e):
            def section(h):
                ci = 0
                di = 0
                for fn, kind, waits in self.ops[e]:
                    for (peng, pkind, pidx) in waits:
                        if pkind == "c":
                            h.wait_ge(csem[peng], cval[peng][pidx])
                        else:
                            h.wait_ge(dsem[peng][pidx % self.NDL],
                                      16 * (pidx // self.NDL + 1))
                    if kind == "d" and di >= self.NDL:
                        h.wait_ge(dsem[e][di % self.NDL],
                                  16 * (di // self.NDL))
                    inst = fn(h)
                    if kind == "d":
                        inst.then_inc(dsem[e][di % self.NDL], 16)
                        di += 1
                    else:
                        if ci in self.referenced[e]:
                            inst.then_inc(csem[e], 1)
                        ci += 1
                for i in range(min(self.NDL, self.didx[e])):
                    cnt = (self.didx[e] - i + self.NDL - 1) // self.NDL
                    h.wait_ge(dsem[e][i], 16 * cnt)
            return section

        block.tensor(make_section("t"))
        block.vector(make_section("v"))
        block.scalar(make_section("s"))
        block.gpsimd(make_section("g"))
        block.sync(make_section("y"))


def rowkeys(base, n=NT):
    return [f"{base}:{i}" for i in range(n)]



def build_l1():
    nc = bass.Bass("TRN2", target_bir_lowering=False)
    X = nc.dram_tensor("x", [T, D], F32, kind="ExternalInput").ap()
    WT = {p: nc.dram_tensor(f"w{p}T", [D, D], F32, kind="ExternalInput").ap()
          for p in "qkvo"}
    G = {p: nc.dram_tensor(f"g{p}", [1, D], F32, kind="ExternalInput").ap()
         for p in "qkvo"}
    H1 = nc.dram_tensor("h1", [T, D], BF16, kind="ExternalOutput").ap()

    R = {p: nc.dram_tensor(f"r_{p}", [T, D], BF16, kind="Internal").ap()
         for p in "qkvo"}
    PD = {"q": nc.dram_tensor("q_tok", [T, D], BF16, kind="Internal").ap(),
          "k": nc.dram_tensor("k_tok", [T, D], BF16, kind="Internal").ap(),
          "v": nc.dram_tensor("v_tok", [T, D], BF16, kind="Internal").ap()}
    OTD = nc.dram_tensor("outT", [D, T], BF16, kind="Internal").ap()
    ZD = nc.dram_tensor("zd", [NH, T], F32, kind="Internal").ap()

    stack = ExitStack()
    with stack:
        SB = {}
        P = Prog(nc)

        uid = [0]

        def sb(st, name, shape, dt=F32):
            uid[0] += 1
            t = st.enter_context(nc.sbuf_tensor(f"{name}_{uid[0]}", shape, dt))
            SB[name] = t
            return t

        def ps(st, name, shape):
            uid[0] += 1
            return st.enter_context(
                nc.psum_tensor(f"{name}_{uid[0]}", shape, F32))

        ones_bf = sb(stack, "ones_bf", [128, 1], BF16)
        ones_fc = sb(stack, "ones_fc", [128, 1])
        ones_row = sb(stack, "ones_row", [1, 128])
        b1285 = sb(stack, "b1285", [128, 1])
        bneg = sb(stack, "bneg", [128, 1])
        for nm in ("c_eps6", "c_eps5", "c_invD", "c_neg05", "c_pos05",
                   "c_smscale"):
            sb(stack, nm, [128, 1])
        for nm, val in (("ones_bf", 1.0), ("ones_fc", 1.0), ("ones_row", 1.0),
                        ("b1285", 128.5), ("bneg", EXP_BIAS),
                        ("c_eps6", 1e-6), ("c_eps5", 1e-5),
                        ("c_invD", 1.0 / D), ("c_neg05", -0.5),
                        ("c_pos05", 0.5), ("c_smscale", SM_SCALE)):
            P.op("v", (lambda nm, val: lambda h: h.memset(SB[nm][:], val))(nm, val),
                 [], [nm])
        for p in "qkvo":
            sb(stack, f"gbc_{p}", [128, D], BF16)
            sb(stack, f"iw_{p}", [128, 1])
            sb(stack, f"thr_{p}", [128, 1])
            sb(stack, f"nthr_{p}", [128, 1])
            sb(stack, f"osc_{p}", [128, NT])
        xb = [sb(stack, f"xb{i}", [128, D]) for i in range(2)]
        iorow = sb(stack, "iorow", [128, 128])
        iocol = sb(stack, "iocol", [128, 128])
        idmat = sb(stack, "idmat", [128, 128])
        P.op("g", lambda h: h.iota(iorow[:], [[1, 128]], base=0,
                                   channel_multiplier=0,
                                   allow_small_or_imprecise_dtypes=True),
             [], ["iorow"])
        P.op("g", lambda h: h.iota(iocol[:], [[0, 128]], base=0,
                                   channel_multiplier=1,
                                   allow_small_or_imprecise_dtypes=True),
             [], ["iocol"])
        P.op("v", lambda h: h.tensor_tensor(idmat[:], iorow[:], iocol[:],
                                            ALU.is_equal),
             ["iorow", "iocol"], ["idmat"])
        wbuf = [sb(stack, f"wbuf{i}", [128, D]) for i in range(2)]
        scr = sb(stack, "scr", [128, D])
        for nm in ("ssq", "lrow", "irms", "rms5", "mx", "mc", "rc", "qs",
                   "acol", "smean", "ssq2", "msq", "mean", "mean2", "var",
                   "rstd"):
            sb(stack, nm, [128, 1])
        tot_sb = sb(stack, "tot_sb", [1, 4])

        st0 = ExitStack()
        pg = ps(st0, "pg", [128, 512])
        ptot = ps(st0, "ptot", [1, 4])
        pbc = ps(st0, "pbc", [128, 4])
        growb = sb(st0, "growb", [1, D])
        for p in "qkvo":
            P.dma("y", (lambda p: lambda h: h.dma_start(SB["growb"][:], G[p]))(p),
                  ["growb_r"], ["growb"])
            for j in range(4):
                P.op("t", (lambda p, j: lambda h: h.matmul(
                    pg[:], ones_row[:], SB["growb"][:, j * 512:(j + 1) * 512],
                    start=True, stop=True))(p, j),
                    ["ones_row", "growb"], ["pg"])
                P.op("v", (lambda p, j: lambda h: h.tensor_copy(
                    SB[f"gbc_{p}"][:, j * 512:(j + 1) * 512], pg[:]))(p, j),
                    ["pg"], [f"gbc_{p}", "growb_r"])
        for pi, p in enumerate("qkvo"):
            sb(st0, f"asum_{p}", [128, NT]) if f"asum_{p}" not in SB else None
            for cc in range(NT):
                b, bk = wbuf[cc % 2], f"wbuf{cc % 2}"
                P.dma("y", (lambda p, cc, b: lambda h: h.dma_start(
                    b[:], WT[p][cc * 128:(cc + 1) * 128, :]))(p, cc, b),
                    [], [bk])
                P.op("s", (lambda p, cc, b: lambda h: h.activation(
                    scr[:], b[:], AF.Abs,
                    accum_out=SB[f"asum_{p}"][:, cc:cc + 1]))(p, cc, b),
                    [bk], ["scr", f"asum_{p}"])
            P.op("v", (lambda p: lambda h: h.reduce_sum(
                SB["acol"][:], SB[f"asum_{p}"][:], AX.X))(p),
                [f"asum_{p}"], ["acol"])
            P.op("t", (lambda pi: lambda h: h.matmul(
                ptot[0:1, pi:pi + 1], SB["acol"][:], ones_fc[:],
                start=True, stop=True))(pi),
                ["acol", "ones_fc"], ["ptot"])
        P.op("v", lambda h: h.tensor_copy(tot_sb[:], ptot[:]),
             ["ptot"], ["tot_sb"])
        for pi in range(4):
            P.op("t", (lambda pi: lambda h: h.matmul(
                pbc[:, pi:pi + 1], ones_row[:], tot_sb[0:1, pi:pi + 1],
                start=True, stop=True))(pi),
                ["tot_sb", "ones_row"], ["pbc"])
        for pi, p in enumerate("qkvo"):
            P.op("v", (lambda p, pi: lambda h: h.tensor_scalar(
                SB[f"iw_{p}"][:], pbc[:, pi:pi + 1], 1.0 / (D * D), 1e-5,
                ALU.mult, ALU.max))(p, pi), ["pbc"], [f"iw_{p}"])
            P.op("v", (lambda p: lambda h: h.tensor_scalar_mul(
                SB[f"thr_{p}"][:], SB[f"iw_{p}"][:], 0.5))(p),
                [f"iw_{p}"], [f"thr_{p}"])
            P.op("v", (lambda p: lambda h: h.tensor_scalar_mul(
                SB[f"nthr_{p}"][:], SB[f"iw_{p}"][:], -0.5))(p),
                [f"iw_{p}"], [f"nthr_{p}"])

        def quant_block(stq, tag):
            for nm, shp, dt in ((f"t1{tag}", [128, D], F32),
                                (f"f{tag}", [128, D], F32),
                                (f"u16{tag}", [128, D], I16),
                                (f"corr{tag}", [128, D], F32),
                                (f"rbuf{tag}", [128, D], BF16),
                                (f"mx{tag}", [128, 1], F32),
                                (f"mc{tag}", [128, 1], F32),
                                (f"rc{tag}", [128, 1], F32),
                                (f"qs{tag}", [128, 1], F32)):
                sb(stq, nm, shp, dt)

        def quant_rows(src_key, src_ap, p, tag, tc, rdst_key):
            t1k = f"t1{tag}"
            mxk, mck, rck, qsk = (f"mx{tag}", f"mc{tag}", f"rc{tag}",
                                  f"qs{tag}")
            P.op("v", (lambda: lambda h: h.tensor_tensor(
                SB[t1k][:], src_ap[:], SB[f"gbc_{p}"][:], ALU.mult))(),
                [src_key, f"gbc_{p}"], [t1k])
            P.op("v", (lambda: lambda h: h.reduce_max(
                SB[mxk][:], SB[t1k][:], AX.X, apply_absolute_value=True))(),
                [t1k], [mxk])
            P.op("v", (lambda: lambda h: h.tensor_tensor(
                SB[mck][:], SB[mxk][:], SB["rms5"][:], ALU.max))(),
                [mxk, "rms5"], [mck])
            P.op("v", (lambda: lambda h: h.reciprocal(
                SB[rck][:], SB[mck][:]))(), [mck], [rck])
            P.op("v", (lambda: lambda h: h.tensor_scalar_mul(
                SB[qsk][:], SB[rck][:], 127.0))(), [rck], [qsk])
            P.op("v", (lambda p, tc: lambda h: h.tensor_tensor(
                SB[f"osc_{p}"][:, tc:tc + 1], SB[mck][:], SB["irms"][:],
                ALU.mult))(p, tc),
                [mck, "irms"], [f"osc_{p}"])
            P.op("s", (lambda: lambda h: h.activation(
                SB[f"f{tag}"][:], SB[t1k][:], AF.Identity,
                bias=b1285[:, 0:1], scale=SB[qsk][:, 0:1]))(),
                [t1k, qsk, "b1285"], [f"f{tag}"])
            P.op("s", (lambda: lambda h: h.activation(
                SB[f"u16{tag}"][:], SB[t1k][:], AF.Identity,
                bias=b1285[:, 0:1], scale=SB[qsk][:, 0:1]))(),
                [t1k, qsk, "b1285"], [f"u16{tag}"])
            P.op("v", (lambda: lambda h: h.tensor_tensor(
                SB[f"corr{tag}"][:], SB[f"u16{tag}"][:], SB[f"f{tag}"][:],
                ALU.is_gt))(),
                [f"u16{tag}", f"f{tag}"], [f"corr{tag}"])
            P.op("v", (lambda: lambda h: h.scalar_tensor_tensor(
                SB[f"rbuf{tag}"][:], SB[f"u16{tag}"][:], -128.0,
                SB[f"corr{tag}"][:], ALU.add, ALU.subtract))(),
                [f"u16{tag}", f"corr{tag}"], [f"rbuf{tag}"])
            P.dma("y", (lambda p, tc: lambda h: h.dma_start(
                R[p][tc * 128:(tc + 1) * 128, :], SB[f"rbuf{tag}"][:]))(p, tc),
                [f"rbuf{tag}"], [f"r_{p}:{tc}"])

        def rms_stats(src_key, src_ap):
            P.op("s", (lambda: lambda h: h.activation(
                scr[:], src_ap[:], AF.Square, accum_out=SB["ssq"][:]))(),
                [src_key], ["scr", "ssq"])
            P.op("s", lambda h: h.activation(SB["lrow"][:], SB["ssq"][:],
                                             AF.Ln, bias=SB["c_eps6"][:, 0:1],
                                             scale=SB["c_invD"][:, 0:1]),
                 ["ssq", "c_eps6", "c_invD"], ["lrow"])
            P.op("s", lambda h: h.activation(SB["irms"][:], SB["lrow"][:],
                                             AF.Exp,
                                             scale=SB["c_neg05"][:, 0:1]),
                 ["lrow", "c_neg05"], ["irms"])
            P.op("s", lambda h: h.activation(SB["rms5"][:], SB["lrow"][:],
                                             AF.Exp,
                                             scale=SB["c_pos05"][:, 0:1]),
                 ["lrow", "c_pos05"], ["rms5_raw"])
            P.op("v", lambda h: h.tensor_scalar_mul(SB["rms5"][:],
                                                    SB["rms5"][:], 1e-5),
                 ["rms5_raw"], ["rms5"])

        st1 = ExitStack()
        quant_block(st1, "A")
        quant_block(st1, "B")
        for tc in range(NT):
            b, bk = xb[tc % 2], f"xb{tc % 2}"
            P.dma("y", (lambda tc, b: lambda h: h.dma_start(
                b[:], X[tc * 128:(tc + 1) * 128, :]))(tc, b), [], [bk])
            rms_stats(bk, b)
            for pj, p in enumerate("qkv"):
                quant_rows(bk, b, p, "AB"[(tc * 3 + pj) % 2], tc, None)
        for p in "qkv":
            P.op("v", (lambda p: lambda h: h.tensor_scalar(
                SB[f"osc_{p}"][:], SB[f"osc_{p}"][:],
                SB[f"iw_{p}"][:, 0:1], 1.0 / 127.0, ALU.mult, ALU.mult))(p),
                [f"osc_{p}", f"iw_{p}"], [f"osc_{p}"])
        P.barrier()
        st1.close()
        st0.close()

        def projection(stp, p, dst, osc_key, epilogue=None):
            rT = [sb(stp, f"rT{i}", [128, D], BF16) for i in range(NT)]
            wt = [sb(stp, f"wt{i}", [128, D], BF16) for i in range(NT)]
            ltm = sb(stp, "ltm", [128, D], BF16)
            pmm = [ps(stp, f"pmm{i}", [128, 512]) for i in range(2)]
            evb = ([sb(stp, f"evb{i}", [128, 512], BF16) for i in range(2)]
                   if epilogue is None else None)
            for cc in range(NT):
                P.dma("y", (lambda p, cc: lambda h: h.dma_start(
                    rT[cc][:], R[p][:, cc * 128:(cc + 1) * 128],
                    transpose=True))(p, cc),
                    rowkeys(f"r_{p}"), [f"rT{cc}"])
            for cc in range(NT):
                b, bk = wbuf[cc % 2], f"wbuf{cc % 2}"
                P.dma("y", (lambda p, cc, b: lambda h: h.dma_start(
                    b[:], WT[p][cc * 128:(cc + 1) * 128, :]))(p, cc, b),
                    [], [bk])
                P.op("v", (lambda p, b: lambda h: h.tensor_single_scalar(
                    ltm[:], b[:], SB[f"nthr_{p}"][:, 0:1], ALU.is_lt))(p, b),
                    [bk, f"nthr_{p}"], ["ltm"])
                P.op("v", (lambda p, cc, b: lambda h: h.scalar_tensor_tensor(
                    wt[cc][:], b[:], SB[f"thr_{p}"][:, 0:1], ltm[:],
                    ALU.is_gt, ALU.subtract))(p, cc, b),
                    [bk, f"thr_{p}", "ltm"], [f"wt{cc}"])
            for tc in range(NT):
                for ob in range(4):
                    pi = (tc * 4 + ob) % 2
                    for cc in range(NT):
                        P.op("t", (lambda tc, ob, cc, pi: lambda h: h.matmul(
                            pmm[pi][:], rT[cc][:, tc * 128:(tc + 1) * 128],
                            wt[cc][:, ob * 512:(ob + 1) * 512],
                            start=(cc == 0), stop=(cc == NT - 1)))(tc, ob, cc, pi),
                            [f"rT{cc}", f"wt{cc}"], [f"pmm{pi}"])
                    if epilogue is None:
                        eb, ek = evb[pi], f"evb{pi}"
                        P.op("s", (lambda tc, pi, eb: lambda h: h.activation(
                            eb[:], pmm[pi][:], AF.Copy,
                            scale=SB[osc_key][:, tc:tc + 1]))(tc, pi, eb),
                            [f"pmm{pi}", osc_key], [ek])
                        P.dma("y", (lambda p, tc, ob, eb: lambda h: h.dma_start(
                            dst[tc * 128:(tc + 1) * 128,
                                ob * 512:(ob + 1) * 512], eb[:]))(p, tc, ob, eb),
                            [ek], [f"dst_{p}:{tc}"])
                    else:
                        epilogue(tc, ob, pi, pmm)

        for p in "kvq":
            stp = ExitStack()
            projection(stp, p, PD[p], f"osc_{p}")
            P.barrier()
            stp.close()

        st2 = ExitStack()
        vres = [sb(st2, f"vres{i}", [128, D], BF16) for i in range(NT)]
        qt = [sb(st2, f"qt{i}", [128, T], BF16) for i in range(2)]
        kt = [sb(st2, f"kt{i}", [128, T], BF16) for i in range(2)]
        expt = [sb(st2, f"expt{i}", [128, 1024], BF16) for i in range(NT)]
        oevb = [sb(st2, f"oevb{i}", [128, 1024], BF16) for i in range(2)]
        psc = [ps(st2, f"psc{i}", [128, 1024]) for i in range(2)]
        pot = ps(st2, "pot", [128, 1024])
        pzr = ps(st2, "pzr", [1, 1024])
        zsb = sb(st2, "zsb", [1, 1024])
        for cc in range(NT):
            P.dma("y", (lambda cc: lambda h: h.dma_start(
                vres[cc][:], PD["v"][cc * 128:(cc + 1) * 128, :]))(cc),
                [f"dst_v:{cc}"], [f"vres{cc}"])
        for hh in range(NH):
            hb = hh % 2
            P.dma("y", (lambda hh, hb: lambda h: h.dma_start(
                qt[hb][:], PD["q"][:, hh * 128:(hh + 1) * 128],
                transpose=True))(hh, hb), rowkeys("dst_q"), [f"qt{hb}"])
            P.dma("y", (lambda hh, hb: lambda h: h.dma_start(
                kt[hb][:], PD["k"][:, hh * 128:(hh + 1) * 128],
                transpose=True))(hh, hb), rowkeys("dst_k"), [f"kt{hb}"])
            for qh in range(2):
                for sc in range(NT):
                    sp = sc % 2
                    for j in range(2):
                        P.op("t", (lambda hb, qh, sc, sp, j: lambda h: h.matmul(
                            psc[sp][:, j * 512:(j + 1) * 512],
                            kt[hb][:, sc * 128:(sc + 1) * 128],
                            qt[hb][:, qh * 1024 + j * 512:
                                    qh * 1024 + (j + 1) * 512],
                            start=True, stop=True))(hb, qh, sc, sp, j),
                            [f"kt{hb}", f"qt{hb}"], [f"psc{sp}"])
                    P.op("s", (lambda sc, sp: lambda h: h.activation(
                        expt[sc][:], psc[sp][:], AF.Exp,
                        bias=bneg[:, 0:1],
                        scale=SB["c_smscale"][:, 0:1]))(sc, sp),
                        [f"psc{sp}", "bneg", "c_smscale"], [f"expt{sc}"])
                    for j in range(2):
                        P.op("t", (lambda hh, sc, j: lambda h: h.matmul(
                            pot[:, j * 512:(j + 1) * 512],
                            vres[sc][:, hh * 128:(hh + 1) * 128],
                            expt[sc][:, j * 512:(j + 1) * 512],
                            start=(sc == 0), stop=(sc == NT - 1)))(hh, sc, j),
                            [f"vres{sc}", f"expt{sc}"], ["pot"])
                for sc in range(NT):
                    for j in range(2):
                        P.op("t", (lambda sc, j: lambda h: h.matmul(
                            pzr[0:1, j * 512:(j + 1) * 512],
                            ones_bf[:],
                            expt[sc][:, j * 512:(j + 1) * 512],
                            start=(sc == 0), stop=(sc == NT - 1)))(sc, j),
                            [f"expt{sc}", "ones_bf"], ["pzr"])
                P.op("v", lambda h: h.tensor_copy(zsb[:], pzr[:]),
                     ["pzr"], ["zsb"])
                P.dma("y", (lambda hh, qh: lambda h: h.dma_start(
                    ZD[hh:hh + 1, qh * 1024:(qh + 1) * 1024], zsb[:]))(hh, qh),
                    ["zsb"], ["zd_d"])
                ob, obk = oevb[qh], f"oevb{qh}"
                P.op("s", (lambda ob: lambda h: h.activation(
                    ob[:], pot[:], AF.Copy))(ob), ["pot"], [obk])
                P.dma("y", (lambda hh, qh, ob: lambda h: h.dma_start(
                    OTD[hh * 128:(hh + 1) * 128, qh * 1024:(qh + 1) * 1024],
                    ob[:]))(hh, qh, ob), [obk], [f"outT_d:{hh}"])
        P.barrier()
        st2.close()

        st3 = ExitStack()
        quant_block(st3, "A")
        zr16 = sb(st3, "zr16", [NH, T])
        zbc = sb(st3, "zbc", [128, NH])
        zb = sb(st3, "zb", [128, D])
        outm = sb(st3, "outm", [128, D], BF16)
        ocin = [sb(st3, f"ocin{i}", [128, D], BF16) for i in range(2)]
        pzt = ps(st3, "pzt", [128, NH])
        P.dma("y", lambda h: h.dma_start(zr16[:], ZD), ["zd_d"], ["zr16_r"])
        P.op("v", lambda h: h.reciprocal(zr16[:], zr16[:]),
             ["zr16_r"], ["zr16"])
        for qc in range(NT):
            b, bk = ocin[qc % 2], f"ocin{qc % 2}"
            P.dma("y", (lambda qc, b: lambda h: h.dma_start(
                b[:], OTD[:, qc * 128:(qc + 1) * 128], transpose=True))(qc, b),
                rowkeys("outT_d", NH), [bk])
            P.op("t", (lambda qc: lambda h: h.transpose(
                pzt[:], zr16[:, qc * 128:(qc + 1) * 128], idmat[0:NH, 0:NH]))(qc),
                ["zr16", "idmat"], ["pzt"])
            P.op("v", lambda h: h.tensor_copy(zbc[:], pzt[:]),
                 ["pzt"], ["zbc"])
            P.op("v", (lambda qc: lambda h: h.tensor_copy(
                zb[:].rearrange("p (a b) -> p a b", b=128),
                zbc[:].unsqueeze(2).broadcast_to([128, NH, 128])))(qc),
                ["zbc"], ["zb"])
            P.op("v", (lambda b: lambda h: h.tensor_tensor(
                outm[:], b[:], zb[:], ALU.mult))(b), [bk, "zb"], ["outm"])
            rms_stats("outm", outm)
            quant_rows("outm", outm, "o", "A", qc, None)
        P.op("v", lambda h: h.tensor_scalar(
            SB["osc_o"][:], SB["osc_o"][:], SB["iw_o"][:, 0:1], 1.0 / 127.0,
            ALU.mult, ALU.mult), ["osc_o", "iw_o"], ["osc_o"])
        P.barrier()
        st3.close()

        st4 = ExitStack()
        projf = sb(st4, "projf", [128, D])
        s1 = wbuf[0]
        h1b = sb(st4, "h1b", [128, D], BF16)

        def o_epilogue(tc, ob, pi, pmm):
            P.op("s", (lambda tc, ob, pi: lambda h: h.activation(
                projf[:, ob * 512:(ob + 1) * 512], pmm[pi][:], AF.Copy,
                scale=SB["osc_o"][:, tc:tc + 1]))(tc, ob, pi),
                [f"pmm{pi}", "osc_o"], ["projf"])
            if ob == 3:
                b, bk = xb[tc % 2], f"xb{tc % 2}"
                P.dma("y", (lambda tc, b: lambda h: h.dma_start(
                    b[:], X[tc * 128:(tc + 1) * 128, :]))(tc, b), [], [bk])
                P.op("v", (lambda b: lambda h: h.tensor_tensor(
                    s1[:], projf[:], b[:], ALU.add))(b),
                    ["projf", bk], ["s1", "wbuf0"])
                P.op("v", lambda h: h.reduce_sum(SB["smean"][:], s1[:], AX.X),
                     ["s1"], ["smean"])
                P.op("s", lambda h: h.activation(scr[:], s1[:], AF.Square,
                                                 accum_out=SB["ssq2"][:]),
                     ["s1"], ["scr", "ssq2"])
                P.op("v", lambda h: h.tensor_scalar_mul(
                    SB["mean"][:], SB["smean"][:], 1.0 / D),
                    ["smean"], ["mean"])
                P.op("v", lambda h: h.tensor_scalar_mul(
                    SB["msq"][:], SB["ssq2"][:], 1.0 / D),
                    ["ssq2"], ["msq"])
                P.op("v", lambda h: h.tensor_tensor(
                    SB["mean2"][:], SB["mean"][:], SB["mean"][:], ALU.mult),
                    ["mean"], ["mean2"])
                P.op("v", lambda h: h.tensor_tensor(
                    SB["var"][:], SB["msq"][:], SB["mean2"][:], ALU.subtract),
                    ["msq", "mean2"], ["var"])
                P.op("s", lambda h: h.activation(SB["lrow"][:], SB["var"][:],
                                                 AF.Ln,
                                                 bias=SB["c_eps5"][:, 0:1]),
                     ["var", "c_eps5"], ["lrow"])
                P.op("s", lambda h: h.activation(SB["rstd"][:], SB["lrow"][:],
                                                 AF.Exp,
                                                 scale=SB["c_neg05"][:, 0:1]),
                     ["lrow", "c_neg05"], ["rstd"])
                P.op("v", lambda h: h.tensor_scalar(
                    h1b[:], s1[:], SB["mean"][:, 0:1], SB["rstd"][:, 0:1],
                    ALU.subtract, ALU.mult),
                    ["s1", "mean", "rstd"], ["h1b"])
                P.dma("y", (lambda tc: lambda h: h.dma_start(
                    H1[tc * 128:(tc + 1) * 128, :], h1b[:]))(tc),
                    ["h1b"], [f"h1_out:{tc}"])

        projection(st4, "o", None, "osc_o", epilogue=o_epilogue)
        P.barrier()
        st4.close()

        P.emit(stack)
    return nc


def l1_inputs(x_b, wq, wk, wv, wo, gq, gk, gv, go):
    return {
        "x": np.ascontiguousarray(x_b, np.float32),
        "wqT": np.ascontiguousarray(wq.T, np.float32),
        "wkT": np.ascontiguousarray(wk.T, np.float32),
        "wvT": np.ascontiguousarray(wv.T, np.float32),
        "woT": np.ascontiguousarray(wo.T, np.float32),
        "gq": np.ascontiguousarray(gq.reshape(1, D), np.float32),
        "gk": np.ascontiguousarray(gk.reshape(1, D), np.float32),
        "gv": np.ascontiguousarray(gv.reshape(1, D), np.float32),
        "go": np.ascontiguousarray(go.reshape(1, D), np.float32),
    }




TT_ALL = 8 * T
NCHUNK = TT_ALL // 128
HB = [512, 512, 512, 512, 512, 256]
NHC = HF // 128


def build_l2():
    nc = bass.Bass("TRN2", target_bir_lowering=False)
    H1F = nc.dram_tensor("h1f", [TT_ALL, D], BF16, kind="ExternalInput").ap()
    H1M = nc.dram_tensor("h1m", [T, D], BF16, kind="ExternalInput").ap()
    WG = nc.dram_tensor("wg", [D, NE], F32, kind="ExternalInput").ap()
    W1 = nc.dram_tensor("w1", [D, HF], F32, kind="ExternalInput").ap()
    W3 = nc.dram_tensor("w3", [D, HF], F32, kind="ExternalInput").ap()
    W2 = nc.dram_tensor("w2", [HF, D], F32, kind="ExternalInput").ap()
    SW1 = nc.dram_tensor("sw1", [D, HF], F32, kind="ExternalInput").ap()
    SW3 = nc.dram_tensor("sw3", [D, HF], F32, kind="ExternalInput").ap()
    SW2 = nc.dram_tensor("sw2", [HF, D], F32, kind="ExternalInput").ap()
    EID = nc.dram_tensor("eid", [1, 1], F32, kind="ExternalInput").ap()

    OE = nc.dram_tensor("oe", [CAP, D], BF16, kind="ExternalOutput").ap()
    PART = nc.dram_tensor("partial", [T, D], BF16, kind="ExternalOutput").ap()
    GK = nc.dram_tensor("gk", [1, TT_ALL], F32, kind="ExternalOutput").ap()
    SLOT = nc.dram_tensor("slot", [1, TT_ALL], F32, kind="ExternalOutput").ap()
    DBG = nc.dram_tensor("dbg", [4, TT_ALL], F32, kind="ExternalOutput").ap()

    S1D = nc.dram_tensor("s1d", [T, HF], BF16, kind="Internal").ap()
    GD = nc.dram_tensor("gd", [T, HF], BF16, kind="Internal").ap()
    YSH = nc.dram_tensor("ysh", [T, D], BF16, kind="Internal").ap()
    XD = nc.dram_tensor("xd", [CAP + 16, D], BF16, kind="Internal").ap()
    LGTD = nc.dram_tensor("lgtd", [NE, TT_ALL], F32, kind="Internal").ap()
    BND = nc.dram_tensor("bnd", [64, 1024], F32, kind="Internal").ap()
    ECOL = nc.dram_tensor("ecol", [NE, 16], F32, kind="Internal").ap()
    TRID = nc.dram_tensor("trid", [16, 16], F32, kind="Internal").ap()
    IDXD = nc.dram_tensor("idxd", [1, TT_ALL], F32, kind="Internal").ap()

    stack = ExitStack()
    with stack:
        SB = {}
        P = Prog(nc)
        uid = [0]

        def sb(st, name, shape, dt=F32):
            uid[0] += 1
            t = st.enter_context(nc.sbuf_tensor(f"{name}_{uid[0]}", shape, dt))
            SB[name] = t
            return t

        def ps(st, name, shape, dt=F32):
            uid[0] += 1
            return st.enter_context(
                nc.psum_tensor(f"{name}_{uid[0]}", shape, dt))

        ones_row8 = sb(stack, "ones_row8", [1, NE])
        iorow = sb(stack, "iorow", [128, 128])
        iocol = sb(stack, "iocol", [128, 128])
        trif = sb(stack, "trif", [128, 128])
        idm = sb(stack, "idm", [128, 128])
        zt = sb(stack, "zt", [128, D], BF16)
        eidt = sb(stack, "eidt", [1, 1])
        iota8 = sb(stack, "iota8", [NE, 16])
        P.op("v", lambda h: h.memset(ones_row8[:], 1.0), [], ["ones_row8"])
        P.op("v", lambda h: h.memset(zt[:], 0.0), [], ["zt"])
        P.op("g", lambda h: h.iota(iorow[:], [[1, 128]], base=0,
                                   channel_multiplier=0,
                                   allow_small_or_imprecise_dtypes=True),
             [], ["iorow"])
        P.op("g", lambda h: h.iota(iocol[:], [[0, 128]], base=0,
                                   channel_multiplier=1,
                                   allow_small_or_imprecise_dtypes=True),
             [], ["iocol"])
        P.op("v", lambda h: h.tensor_tensor(trif[:], iocol[:], iorow[:],
                                            ALU.is_lt),
             ["iorow", "iocol"], ["trif"])
        P.op("v", lambda h: h.tensor_tensor(idm[:], iocol[:], iorow[:],
                                            ALU.is_equal),
             ["iorow", "iocol"], ["idm"])
        P.op("g", lambda h: h.iota(iota8[:], [[0, 16]], base=0,
                                   channel_multiplier=1,
                                   allow_small_or_imprecise_dtypes=True),
             [], ["iota8"])
        P.dma("y", lambda h: h.dma_start(eidt[:], EID), [], ["eidt"])
        P.dma("g", lambda h: h.dma_start(ECOL, iota8[:]), ["iota8"], ["ecold"])
        P.dma("g", lambda h: h.dma_start(TRID, trif[0:16, 0:16]),
             ["trif"], ["trid_d"])

        def ffn(tag, XT_dram, xt_rows, W1d, W3d, W2d, DST, dstkeys,
                whalves=1, w2div=None):
            if w2div is None:
                w2div = whalves
            stx = ExitStack()
            xT = [sb(stx, f"xT{tag}{i}", [128, xt_rows], BF16)
                  for i in range(NT)]
            hwid = HF // whalves
            wres = [sb(stx, f"wres{tag}{i}", [128, hwid], BF16)
                    for i in range(NT)]
            sgb = [sb(stx, f"sgb{tag}{i}", [128, 512], BF16) for i in range(2)]
            s1b = [sb(stx, f"s1b{tag}{i}", [128, 512], BF16) for i in range(2)]
            pfm = [ps(stx, f"pfm{tag}{i}", [128, 512]) for i in range(2)]
            ntc = xt_rows // 128
            if whalves == 1:
                HBL = [(0, HB)]
            else:
                HBL = [(0, [512, 512, 384]), (1408, [512, 512, 384])]
            for cc in range(NT):
                P.dma("y", (lambda cc: lambda h: h.dma_start(
                    xT[cc][:], XT_dram[0:xt_rows, cc * 128:(cc + 1) * 128],
                    transpose=True))(cc),
                    dstkeys, [f"xT{tag}{cc}"])
            for phase, WD in (("a", W1d), ("b", W3d)):
                for hoff, HBs in HBL:
                    for cc in range(NT):
                        P.dma("g", (lambda cc, WD, hoff: lambda h: h.dma_start(
                            wres[cc][:],
                            WD[cc * 128:(cc + 1) * 128,
                               hoff:hoff + hwid]))(cc, WD, hoff),
                            [f"wres{tag}{cc}"], [f"wres{tag}{cc}"])
                    for tc in range(ntc):
                        off = 0
                        for hbw in HBs:
                            pi = (tc * len(HBs) + (off // 512)) % 2
                            goff = hoff + off
                            for cc in range(NT):
                                P.op("t", (lambda tc, off, hbw, cc, pi:
                                           lambda h: h.matmul(
                                    pfm[pi][:, 0:hbw],
                                    xT[cc][:, tc * 128:(tc + 1) * 128],
                                    wres[cc][:, off:off + hbw],
                                    start=(cc == 0), stop=(cc == NT - 1)))(
                                        tc, off, hbw, cc, pi),
                                    [f"xT{tag}{cc}", f"wres{tag}{cc}"],
                                    [f"pfm{tag}{pi}"])
                            if phase == "a":
                                eb, ek = sgb[pi], f"sgb{tag}{pi}"
                                P.op("s", (lambda pi, hbw, eb: lambda h:
                                           h.activation(
                                    eb[:, 0:hbw], pfm[pi][:, 0:hbw],
                                    AF.Silu))(pi, hbw, eb),
                                    [f"pfm{tag}{pi}"], [ek])
                                P.dma("y", (lambda tc, goff, hbw, eb:
                                            lambda h: h.dma_start(
                                    S1D[tc * 128:(tc + 1) * 128,
                                        goff:goff + hbw],
                                    eb[:, 0:hbw]))(tc, goff, hbw, eb),
                                    [ek], [f"s1d{tag}:{tc}"])
                            else:
                                lb, lk = s1b[pi], f"s1b{tag}{pi}"
                                P.dma("y", (lambda tc, goff, hbw, lb:
                                            lambda h: h.dma_start(
                                    lb[:, 0:hbw],
                                    S1D[tc * 128:(tc + 1) * 128,
                                        goff:goff + hbw]))(tc, goff, hbw, lb),
                                    [f"s1d{tag}:{tc}"], [lk])
                                eb, ek = sgb[pi], f"sgb{tag}{pi}"
                                P.op("v", (lambda pi, hbw, lb, eb: lambda h:
                                           h.tensor_tensor(
                                    eb[:, 0:hbw], lb[:, 0:hbw],
                                    pfm[pi][:, 0:hbw], ALU.mult))(
                                        pi, hbw, lb, eb),
                                    [f"pfm{tag}{pi}", lk], [ek])
                                P.dma("y", (lambda tc, goff, hbw, eb:
                                            lambda h: h.dma_start(
                                    GD[tc * 128:(tc + 1) * 128,
                                       goff:goff + hbw],
                                    eb[:, 0:hbw]))(tc, goff, hbw, eb),
                                    [ek], [f"gd{tag}:{tc}"])
                            off += hbw
            P.barrier()
            stx.close()
            sty = ExitStack()
            gT = [sb(sty, f"gT{tag}{i}", [128, xt_rows], BF16)
                  for i in range(NHC)]
            dwid = D // w2div
            w2r = [sb(sty, f"w2r{tag}{i}", [128, dwid], BF16)
                   for i in range(NHC)]
            oeb = [sb(sty, f"oeb{tag}{i}", [128, 512], BF16) for i in range(2)]
            pfc = [ps(sty, f"pfc{tag}{i}", [128, 512]) for i in range(2)]
            for hc in range(NHC):
                P.dma("y", (lambda hc: lambda h: h.dma_start(
                    gT[hc][:], GD[0:xt_rows, hc * 128:(hc + 1) * 128],
                    transpose=True))(hc),
                    rowkeys(f"gd{tag}", ntc), [f"gT{tag}{hc}"])
            for dhalf in range(w2div):
                doff = dhalf * dwid
                for hc in range(NHC):
                    P.dma("g", (lambda hc, doff: lambda h: h.dma_start(
                        w2r[hc][:],
                        W2d[hc * 128:(hc + 1) * 128,
                            doff:doff + dwid]))(hc, doff),
                        [f"w2r{tag}{hc}"], [f"w2r{tag}{hc}"])
                for tc in range(ntc):
                    for ob in range(dwid // 512):
                        pi = (tc * (dwid // 512) + ob) % 2
                        for hc in range(NHC):
                            P.op("t", (lambda tc, ob, hc, pi: lambda h:
                                       h.matmul(
                                pfc[pi][:],
                                gT[hc][:, tc * 128:(tc + 1) * 128],
                                w2r[hc][:, ob * 512:(ob + 1) * 512],
                                start=(hc == 0), stop=(hc == NHC - 1)))(
                                    tc, ob, hc, pi),
                                [f"gT{tag}{hc}", f"w2r{tag}{hc}"],
                                [f"pfc{tag}{pi}"])
                        eb, ek = oeb[pi], f"oeb{tag}{pi}"
                        P.op("s", (lambda pi, eb: lambda h: h.activation(
                            eb[:], pfc[pi][:], AF.Copy))(pi, eb),
                            [f"pfc{tag}{pi}"], [ek])
                        P.dma("y", (lambda tc, ob, doff, eb: lambda h:
                                    h.dma_start(
                            DST[tc * 128:(tc + 1) * 128,
                                doff + ob * 512:doff + (ob + 1) * 512],
                            eb[:]))(tc, ob, doff, eb),
                            [ek], [f"ffdst{tag}:{tc}"])
            P.barrier()
            sty.close()

        import os as _os
        _SKIPA = _os.environ.get("SKIPA")
        _SKIPB = _os.environ.get("SKIPB")
        stA = ExitStack()
        h1t = [sb(stA, f"h1t{i}", [128, 1024], BF16) for i in range(NT)]
        wgt = [sb(stA, f"wgt{i}", [128, NE], BF16) for i in range(NT)]
        lgb = [sb(stA, f"lgb{i}", [NE, 512]) for i in range(2)]
        plg = ps(stA, "plg", [NE, 512])
        for cc in range(NT):
            P.dma("g", (lambda cc: lambda h: h.dma_start(
                wgt[cc][:], WG[cc * 128:(cc + 1) * 128, :]))(cc),
                [], [f"wgt{cc}"])
        for sq in range(2 * NE):
            for cc in range(NT):
                P.dma("y", (lambda sq, cc: lambda h: h.dma_start(
                    h1t[cc][:],
                    H1F[sq * 1024:(sq + 1) * 1024,
                        cc * 128:(cc + 1) * 128],
                    transpose=True))(sq, cc),
                    [], [f"h1t{cc}"])
            for jb in range(2):
                for cc in range(NT):
                    P.op("t", (lambda sq, jb, cc: lambda h: h.matmul(
                        plg[:], wgt[cc][:],
                        h1t[cc][:, jb * 512:(jb + 1) * 512],
                        start=(cc == 0), stop=(cc == NT - 1)))(sq, jb, cc),
                        [f"wgt{cc}", f"h1t{cc}"], ["plg"])
                lb, lk = lgb[jb % 2], f"lgb{jb % 2}"
                P.op("v", (lambda lb: lambda h: h.tensor_copy(
                    lb[:], plg[:]))(lb), ["plg"], [lk])
                P.dma("y", (lambda sq, jb, lb: lambda h: h.dma_start(
                    LGTD[:, sq * 1024 + jb * 512: sq * 1024 + (jb + 1) * 512],
                    lb[:]))(sq, jb, lb), [lk], ["lgtd"])
        P.barrier()
        stA.close()

        stB = ExitStack()
        lgP = sb(stB, "lgP", [128, 1024])
        mbP = sb(stB, "mbP", [128, 1024])
        ohP = sb(stB, "ohP", [128, 1024])
        exP = sb(stB, "exP", [128, 1024])
        cumA = sb(stB, "cumA", [128, 1024])
        prodP = exP
        cumB = mbP
        f64 = mbP.ap() if False else mbP
        fsc = sb(stB, "fsc", [64, 1024])
        m16 = sb(stB, "m16", [16, 1024])
        mxs = sb(stB, "mxs", [16, 1024])
        se16 = sb(stB, "se16", [16, 1024])
        e16 = sb(stB, "e16", [16, 1024])
        pos16 = sb(stB, "pos16", [16, 1024])
        keep16 = sb(stB, "keep16", [16, 1024])
        gate16 = sb(stB, "gate16", [16, 1024])
        gk16 = sb(stB, "gk16", [16, 1024])
        slot16 = sb(stB, "slot16", [16, 1024])
        val16 = sb(stB, "val16", [16, 1024])
        idx16 = sb(stB, "idx16", [16, 1024])
        ecol = sb(stB, "ecol_s", [128, 1])
        eidc = sb(stB, "eidc", [16, 1])
        tot = sb(stB, "tot", [128, 1])
        offs = sb(stB, "offs", [128, 1])
        mblk = sb(stB, "mblk", [128, 128])
        idxm = sb(stB, "idxm", [128, 128])
        idxmT = sb(stB, "idxmT", [128, 128])
        idxi = sb(stB, "idxi", [128, 128], I32)
        poffs = ps(stB, "poffs", [128, 1])
        ptr = ps(stB, "ptr", [128, 128])

        P.dma("g", lambda h: h.dma_start(
            ecol[:], ECOL.rearrange("e s -> (e s)").unsqueeze(1)),
            ["ecold"], ["ecol_s"])
        P.dma("g", lambda h: h.dma_start(
            eidc[:], EID.broadcast_to([16, 1])),
            ["eidt"], ["eidc"])
        P.op("v", lambda h: h.memset(mblk[:], 0.0), [], ["mblk_z"])
        for e in range(NE):
            P.dma("g", (lambda e: lambda h: h.dma_start(
                mblk[e * 16:(e + 1) * 16, e * 16:(e + 1) * 16], TRID))(e),
                ["trid_d", "mblk_z"], ["mblk"])
        P.dma("g", lambda h: h.dma_start(
            lgP[:], LGTD.rearrange("e (s i) -> (e s) i", i=1024)),
            ["lgtd"], ["lgP"])

        def fold(op, src, srck, dstk_pref):
            P.dma("g", (lambda: lambda h: h.dma_start(
                BND, src[64:128, :]))(), [srck], ["bndd"])
            P.dma("g", (lambda: lambda h: h.dma_start(
                fsc[:], BND))(), ["bndd"], ["fsc"])
            P.op("v", (lambda: lambda h: h.tensor_tensor(
                f64[0:64, :], src[0:64, :], fsc[:], op))(),
                [srck, "fsc"], [dstk_pref + "64", "mbP"])
            P.dma("g", (lambda: lambda h: h.dma_start(
                BND[0:32, :], f64[32:64, :]))(),
                [dstk_pref + "64", "mbP"], ["bndd"])
            P.dma("g", (lambda: lambda h: h.dma_start(
                fsc[0:32, :], BND[0:32, :]))(), ["bndd"], ["fsc"])
            P.op("v", (lambda: lambda h: h.tensor_tensor(
                f64[0:32, :], f64[0:32, :], fsc[0:32, :], op))(),
                [dstk_pref + "64", "fsc", "mbP"], [dstk_pref + "32", "mbP"])
            P.dma("g", (lambda: lambda h: h.dma_start(
                BND[0:16, :], f64[16:32, :]))(),
                [dstk_pref + "32", "mbP"], ["bndd"])
            P.dma("g", (lambda: lambda h: h.dma_start(
                fsc[0:16, :], BND[0:16, :]))(), ["bndd"], ["fsc"])
            P.op("v", (lambda: lambda h: h.tensor_tensor(
                m16[:], f64[0:16, :], fsc[0:16, :], op))(),
                [dstk_pref + "32", "fsc", "mbP"], [dstk_pref + "16"])

        fold(ALU.max, lgP, "lgP", "mx")
        P.op("v", lambda h: h.tensor_copy(mxs[:], m16[:]),
             ["mx16"], ["m16f"])
        P.dma("g", lambda h: h.dma_start(BND[0:16, :], mxs[:]), ["m16f"], ["bnd2"])
        for e in range(NE):
            P.dma("g", (lambda e: lambda h: h.dma_start(
                mbP[e * 16:(e + 1) * 16, :], BND[0:16, :]))(e),
                ["bnd2"], ["mbP"])
        P.op("v", lambda h: h.tensor_tensor(ohP[:], lgP[:], mbP[:],
                                            ALU.is_ge),
             ["lgP", "mbP"], ["ohP"])
        P.op("s", lambda h: h.activation(exP[:], lgP[:], AF.Exp),
             ["lgP"], ["exP"])
        fold(ALU.add, exP, "exP", "sx")
        P.op("v", lambda h: h.tensor_copy(se16[:], m16[:]),
             ["sx16"], ["se16"])
        P.op("v", lambda h: h.tensor_scalar_mul(prodP[:], ohP[:],
                                                ecol[:, 0:1]),
             ["ohP", "ecol_s"], ["prodP", "exP"])
        fold(ALU.add, prodP, "prodP", "ex")
        P.op("v", lambda h: h.tensor_copy(e16[:], m16[:]),
             ["ex16"], ["e16"])
        P.op("v", lambda h: h.tensor_copy(cumA[:], ohP[:]), ["ohP"], ["cumA"])
        cur, curk, oth, othk = cumA, "cumA", cumB, "cumB"
        P.op("v", lambda h: h.tensor_copy(cumB[:, 0:1], cumA[:, 0:1]),
             ["cumA", "mbP"], ["cumB", "mbP"])
        shift = 1
        while shift < 1024:
            P.op("v", (lambda cur, oth, shift: lambda h: h.tensor_copy(
                oth[:, 0:shift], cur[:, 0:shift]))(cur, oth, shift),
                [curk], [othk])
            P.op("v", (lambda cur, oth, shift: lambda h: h.tensor_tensor(
                oth[:, shift:], cur[:, shift:], cur[:, 0:1024 - shift],
                ALU.add))(cur, oth, shift),
                [curk], [othk])
            cur, curk, oth, othk = oth, othk, cur, curk
            shift *= 2
        P.op("v", (lambda cur: lambda h: h.tensor_copy(
            tot[:], cur[:, 1023:1024]))(cur), [curk], ["tot"])
        P.op("t", lambda h: h.matmul(poffs[:], mblk[:], tot[:],
                                     start=True, stop=True),
             ["mblk", "tot"], ["poffs"])
        P.op("v", lambda h: h.tensor_copy(offs[:], poffs[:]),
             ["poffs"], ["offs"])
        P.op("v", (lambda cur: lambda h: h.tensor_scalar_add(
            cur[:], cur[:], offs[:, 0:1]))(cur),
            [curk, "offs"], [curk + "g"])
        P.op("v", (lambda cur: lambda h: h.tensor_tensor(
            prodP[:], ohP[:], cur[:], ALU.mult))(cur),
            ["ohP", curk + "g"], ["prodP2", "exP"])
        fold(ALU.add, prodP, "prodP2", "ps")
        P.op("v", lambda h: h.tensor_copy(pos16[:], m16[:]),
             ["ps16"], ["pos16"])
        P.op("v", lambda h: h.tensor_single_scalar(
            keep16[:], pos16[:], CAP + 0.5, ALU.is_lt),
            ["pos16"], ["keep16"])
        P.op("s", lambda h: h.activation(gate16[:], mxs[:], AF.Exp),
             ["m16f"], ["gate_e"])
        P.op("v", lambda h: h.reciprocal(val16[:], se16[:]),
             ["se16"], ["rse"])
        P.op("v", lambda h: h.tensor_tensor(gate16[:], gate16[:],
                                            val16[:], ALU.mult),
             ["gate_e", "rse"], ["gate16"])
        P.op("v", lambda h: h.tensor_tensor(gk16[:], gate16[:], keep16[:],
                                            ALU.mult),
             ["gate16", "keep16"], ["gk16"])
        P.op("v", lambda h: h.scalar_tensor_tensor(
            slot16[:], e16[:], float(CAP), pos16[:], ALU.mult, ALU.add),
            ["e16", "pos16"], ["slot_a"])
        P.op("v", lambda h: h.tensor_scalar(
            slot16[:], slot16[:], -1.0, float(TT_ALL - 1),
            ALU.add, ALU.min), ["slot_a"], ["slot16"])
        P.op("v", lambda h: h.tensor_single_scalar(
            val16[:], e16[:], eidc[:, 0:1], ALU.is_equal),
            ["e16", "eidc"], ["val_a"])
        P.op("v", lambda h: h.tensor_tensor(val16[:], val16[:], keep16[:],
                                            ALU.mult),
             ["val_a", "keep16"], ["val16"])
        P.op("v", lambda h: h.tensor_single_scalar(
            idx16[:], pos16[:], -1.0 - CAP, ALU.add),
            ["pos16"], ["idx_a"])
        P.op("v", lambda h: h.tensor_tensor(idx16[:], idx16[:], val16[:],
                                            ALU.mult),
             ["idx_a", "val16"], ["idx_b"])
        P.op("v", lambda h: h.tensor_single_scalar(
            idx16[:], idx16[:], float(CAP), ALU.add),
            ["idx_b"], ["idx16"])
        P.dma("y", lambda h: h.dma_start(
            GK.rearrange("o (s i) -> (o s) i", i=1024), gk16[:]),
            ["gk16"], ["gk_o"])
        P.dma("y", lambda h: h.dma_start(
            DBG[0:1, :].rearrange("o (s i) -> (o s) i", i=1024), mxs[:]),
            ["m16f"], ["dbg_o"])
        P.dma("y", lambda h: h.dma_start(
            DBG[1:2, :].rearrange("o (s i) -> (o s) i", i=1024), e16[:]),
            ["e16"], ["dbg_o"])
        P.dma("y", lambda h: h.dma_start(
            DBG[2:3, :].rearrange("o (s i) -> (o s) i", i=1024), pos16[:]),
            ["pos16"], ["dbg_o"])
        P.dma("y", lambda h: h.dma_start(
            DBG[3:4, 0:128].rearrange("o (s i) -> (o s) i", i=1),
            tot[:]), ["tot"], ["dbg_o"])
        P.dma("y", lambda h: h.dma_start(
            DBG[3:4, 128:256].rearrange("o (s i) -> (o s) i", i=1),
            offs[:]), ["offs"], ["dbg_o"])
        P.dma("y", lambda h: h.dma_start(
            SLOT.rearrange("o (s i) -> (o s) i", i=1024), slot16[:]),
            ["slot16"], ["slot_o"])
        P.dma("g", lambda h: h.dma_start(
            IDXD.rearrange("o (s i) -> (o s) i", i=1024), idx16[:]),
            ["idx16"], ["idxd"])
        P.dma("g", lambda h: h.dma_start(
            idxm[:], IDXD.rearrange("o (p j) -> (o p) j", p=128)),
            ["idxd"], ["idxm"])
        P.op("t", lambda h: h.transpose(ptr[:], idxm[:], idm[:]),
             ["idxm", "idm"], ["ptr"])
        P.op("v", lambda h: h.tensor_copy(idxmT[:], ptr[:]), ["ptr"], ["idxmT"])
        P.op("v", lambda h: h.tensor_copy(idxi[:], idxmT[:]),
             ["idxmT"], ["idxi"])

        for tc in range(NT):
            P.dma("y", (lambda tc: lambda h: h.dma_start(
                XD[tc * 128:(tc + 1) * 128, :], zt[:]))(tc),
                ["zt"], [f"xdz:{tc}"])
        hch = [sb(stB, f"hch{i}", [128, D], BF16) for i in range(2)]
        for j in range(NCHUNK):
            b, bk = hch[j % 2], f"hch{j % 2}"
            P.dma("g", (lambda j, b: lambda h: h.dma_start(
                b[:], H1F[j * 128:(j + 1) * 128, :]))(j, b), [], [bk])
            P.dma("g", (lambda j, b: lambda h: h.indirect_dma_start(
                XD, bass.IndirectOffsetOnAxis(idxi[:, j:j + 1], 0),
                b[:], None))(j, b),
                [bk, "idxi"] + rowkeys("xdz"), ["xd_d"])

        import os as _os2
        if not _os2.environ.get("OVERLAP_S"):
            P.barrier()
            stB.close()
            ffn("S", H1M, T, SW1, SW3, SW2, YSH, [], whalves=2, w2div=4)
        else:
            ffn("S", H1M, T, SW1, SW3, SW2, YSH, [], whalves=2, w2div=4)
            P.barrier()
            stB.close()

        stP = ExitStack()
        pa = sb(stP, "pa", [128, D], BF16)
        pb = sb(stP, "pb", [128, D], BF16)
        pc_ = sb(stP, "pc_", [128, D], BF16)
        for tc in range(NT):
            P.dma("y", (lambda tc: lambda h: h.dma_start(
                pa[:], H1M[tc * 128:(tc + 1) * 128, :]))(tc), ["pa_r"], ["pa"])
            P.dma("y", (lambda tc: lambda h: h.dma_start(
                pb[:], YSH[tc * 128:(tc + 1) * 128, :]))(tc),
                [f"ffdstS:{tc}", "pb_r"], ["pb"])
            P.op("v", lambda h: h.tensor_tensor(pc_[:], pa[:], pb[:],
                                                ALU.add),
                 ["pa", "pb"], ["pc_", "pa_r", "pb_r"])
            P.dma("y", (lambda tc: lambda h: h.dma_start(
                PART[tc * 128:(tc + 1) * 128, :], pc_[:]))(tc),
                ["pc_"], ["part_o"])
        P.barrier()
        stP.close()

        ffn("E", XD, CAP, W1, W3, W2, OE, ["xd_d"], whalves=1)

        P.emit(stack)
    return nc




def build_l3():
    nc = bass.Bass("TRN2", target_bir_lowering=False)
    OEF = nc.dram_tensor("oef", [TT_ALL, D], BF16, kind="ExternalInput").ap()
    PART = nc.dram_tensor("partial", [T, D], BF16, kind="ExternalInput").ap()
    GKM = nc.dram_tensor("gkm", [128, NT], F32, kind="ExternalInput").ap()
    SLM = nc.dram_tensor("slm", [128, NT], F32, kind="ExternalInput").ap()
    Y = nc.dram_tensor("y", [T, D], F32, kind="ExternalOutput").ap()

    stack = ExitStack()
    with stack:
        SB = {}
        P = Prog(nc)

        def sb(name, shape, dt=F32):
            t = stack.enter_context(nc.sbuf_tensor(name, shape, dt))
            SB[name] = t
            return t

        gkm = sb("gkm_s", [128, NT])
        slm = sb("slm_s", [128, NT])
        sli = sb("sli", [128, NT], I32)
        moe = sb("moe", [128, D], BF16)
        par = sb("par", [128, D], BF16)
        s2 = sb("s2", [128, D])
        scr = sb("scr", [128, D])
        yb = sb("yb", [128, D])
        c_eps5 = sb("c_eps5", [128, 1])
        c_neg05 = sb("c_neg05", [128, 1])
        for nm, val in (("c_eps5", 1e-5), ("c_neg05", -0.5)):
            P.op("v", (lambda nm, val: lambda h: h.memset(SB[nm + "_s"]
                 if nm + "_s" in SB else SB[nm][:], val))(nm, val), [], [nm])
        for nm in ("ssq2", "smean", "msq", "mean", "mean2", "var", "lrow",
                   "rstd"):
            sb(nm, [128, 1])
        P.dma("y", lambda h: h.dma_start(gkm[:], GKM), [], ["gkm"])
        P.dma("y", lambda h: h.dma_start(slm[:], SLM), [], ["slm"])
        P.op("v", lambda h: h.tensor_copy(sli[:], slm[:]), ["slm"], ["sli"])
        for tc in range(NT):
            P.dma("g", (lambda tc: lambda h: h.indirect_dma_start(
                moe[:], None, OEF,
                bass.IndirectOffsetOnAxis(sli[:, tc:tc + 1], 0)))(tc),
                ["sli", "moe_r"], ["moe"])
            P.dma("y", (lambda tc: lambda h: h.dma_start(
                par[:], PART[tc * 128:(tc + 1) * 128, :]))(tc), [], ["par"])
            P.op("v", (lambda tc: lambda h: h.scalar_tensor_tensor(
                s2[:], moe[:], gkm[:, tc:tc + 1], par[:],
                ALU.mult, ALU.add))(tc),
                ["moe", "gkm", "par"], ["s2", "moe_r"])
            P.op("v", lambda h: h.reduce_sum(SB["smean"][:], s2[:], AX.X),
                 ["s2"], ["smean"])
            P.op("s", lambda h: h.activation(scr[:], s2[:], AF.Square,
                                             accum_out=SB["ssq2"][:]),
                 ["s2"], ["scr", "ssq2"])
            P.op("v", lambda h: h.tensor_scalar_mul(SB["mean"][:],
                                                    SB["smean"][:], 1.0 / D),
                 ["smean"], ["mean"])
            P.op("v", lambda h: h.tensor_scalar_mul(SB["msq"][:],
                                                    SB["ssq2"][:], 1.0 / D),
                 ["ssq2"], ["msq"])
            P.op("v", lambda h: h.tensor_tensor(SB["mean2"][:], SB["mean"][:],
                                                SB["mean"][:], ALU.mult),
                 ["mean"], ["mean2"])
            P.op("v", lambda h: h.tensor_tensor(SB["var"][:], SB["msq"][:],
                                                SB["mean2"][:], ALU.subtract),
                 ["msq", "mean2"], ["var"])
            P.op("s", lambda h: h.activation(SB["lrow"][:], SB["var"][:],
                                             AF.Ln,
                                             bias=SB["c_eps5"][:, 0:1]),
                 ["var", "c_eps5"], ["lrow"])
            P.op("s", lambda h: h.activation(SB["rstd"][:], SB["lrow"][:],
                                             AF.Exp,
                                             scale=SB["c_neg05"][:, 0:1]),
                 ["lrow", "c_neg05"], ["rstd"])
            P.op("v", lambda h: h.tensor_scalar(
                yb[:], s2[:], SB["mean"][:, 0:1], SB["rstd"][:, 0:1],
                ALU.subtract, ALU.mult),
                ["s2", "mean", "rstd"], ["yb"])
            P.dma("y", (lambda tc: lambda h: h.dma_start(
                Y[tc * 128:(tc + 1) * 128, :], yb[:]))(tc),
                ["yb"], ["y_o"])
        P.emit(stack)
    return nc



_NC_CACHE = {}


def _get(name, builder):
    if name not in _NC_CACHE:
        _NC_CACHE[name] = builder()
    return _NC_CACHE[name]


def kernel(L0_attn_x, L0_attn_wq, L0_attn_wk, L0_attn_wv, L0_attn_wo,
           L0_attn_gq, L0_attn_gk, L0_attn_gv, L0_attn_go,
           L0_ffn_Wg, L0_ffn_W1, L0_ffn_W3, L0_ffn_W2,
           L0_ffn_sw1, L0_ffn_sw3, L0_ffn_sw2):
    x = np.asarray(L0_attn_x)
    cores = list(range(8))

    nc1 = _get("l1", build_l1)
    in1 = [l1_inputs(x[b], L0_attn_wq, L0_attn_wk, L0_attn_wv, L0_attn_wo,
                     L0_attn_gq, L0_attn_gk, L0_attn_gv, L0_attn_go)
           for b in range(8)]
    r1 = run_bass_kernel_spmd(nc1, in1, core_ids=cores)
    h1f = np.concatenate([np.asarray(r1.results[b]["h1"]) for b in range(8)],
                         axis=0)

    nc2 = _get("l2", build_l2)
    in2 = []
    for e in range(8):
        in2.append({
            "h1f": h1f,
            "h1m": np.ascontiguousarray(h1f[e * T:(e + 1) * T]),
            "wg": np.ascontiguousarray(L0_ffn_Wg, np.float32),
            "w1": np.ascontiguousarray(L0_ffn_W1[e], np.float32),
            "w3": np.ascontiguousarray(L0_ffn_W3[e], np.float32),
            "w2": np.ascontiguousarray(L0_ffn_W2[e], np.float32),
            "sw1": np.ascontiguousarray(L0_ffn_sw1, np.float32),
            "sw3": np.ascontiguousarray(L0_ffn_sw3, np.float32),
            "sw2": np.ascontiguousarray(L0_ffn_sw2, np.float32),
            "eid": np.full((1, 1), float(e), np.float32),
        })
    r2 = run_bass_kernel_spmd(nc2, in2, core_ids=cores)
    oef = np.concatenate([np.asarray(r2.results[e]["oe"]) for e in range(8)],
                         axis=0)
    gk = np.asarray(r2.results[0]["gk"]).reshape(TT_ALL)
    slot = np.asarray(r2.results[0]["slot"]).reshape(TT_ALL)

    nc3 = _get("l3", build_l3)
    in3 = []
    for b in range(8):
        gkb = gk[b * T:(b + 1) * T].reshape(NT, 128).T
        slb = slot[b * T:(b + 1) * T].reshape(NT, 128).T
        in3.append({
            "oef": oef,
            "partial": np.asarray(r2.results[b]["partial"]),
            "gkm": np.ascontiguousarray(gkb, np.float32),
            "slm": np.ascontiguousarray(slb, np.float32),
        })
    r3 = run_bass_kernel_spmd(nc3, in3, core_ids=cores)
    y = np.stack([np.asarray(r3.results[b]["y"]) for b in range(8)], axis=0)
    return y
